# version 11 (final)
# speedup vs baseline: 1.3977x; 1.1025x over previous
import numpy as np
import ml_dtypes
from contextlib import ExitStack
import concourse.bass as bass
import concourse.mybir as mybir
from concourse.bass_utils import run_bass_kernel_spmd

F32 = mybir.dt.float32
BF16 = mybir.dt.bfloat16
U32 = mybir.dt.uint32
AF = mybir.ActivationFunctionType
ALU = mybir.AluOpType
AX = mybir.AxisListType

NCORES = 8
S = 8192
T = 1024
TH = T + 3
D = 2048
DI = 4096
DS = 16
DTR = 128
E0 = 8
F0 = 2048
R0 = 64
NL = 2
H = 16
HPC = 2
A1 = 64
R1 = 16
DK = 128
DNOPE = 64
DROPE = 64
DCKV = 512
DCQ = 1024
LN_EPS = 1e-5
RMS_EPS = 1.1920929e-07
GOLDEN_CENTER = float(1.0 / np.e)
GOLDEN_LOWER = float(0.5 - np.log(4.0 / 3.0))
GOLDEN_UPPER = 0.5
CPC = DI // NCORES
NDT = CPC // 128
TTS = 512
NTT = S // TTS
NQT = S // 128

ENGINES = ("sync", "gpsimd", "tensor", "vector", "scalar")
ARENA_COLS = 53152
PRE_ROWS = DI + 32


class Sem:
    def __init__(self, ctx, nc, name):
        self.h = ctx.enter_context(nc.semaphore(name))
        self.n = 0

    def bump(self, k=1):
        self.n += k
        return self.n


class Q:
    def __init__(self):
        self.fns = {k: [] for k in ENGINES}

    def emit(self, block):
        for name in ENGINES:
            fns = self.fns[name]
            if not fns:
                continue
            getattr(block, name)(lambda eng, fns=fns: [f(eng) for f in fns])


def on(q, eng):
    def deco(f):
        q.fns[eng].append(f)
        return None
    return deco


class Ctx:

    def __init__(self, nc, es):
        self.nc = nc
        self.es = es
        self.q = Q()
        self.arena = es.enter_context(nc.sbuf_tensor("arena", [128, ARENA_COLS], F32))
        self.psum = es.enter_context(nc.psum_tensor("psum_all", [128, 4096], F32))
        self.cur = 0
        self.pcur = 0
        self.semn = 0
        self.s_bar = Sem(es, nc, "s_bar")
        self.stage = 0

    def reset(self):
        self.cur = 0
        self.pcur = 0

    def sb(self, shape, dt):
        p, cols = shape
        f32c = cols if dt == F32 else (cols * 2 + 3) // 4 if dt == BF16 else cols
        if dt == BF16:
            f32c = (cols + 1) // 2
        base = self.cur
        self.cur += f32c
        assert self.cur <= ARENA_COLS, f"arena overflow {self.cur}"
        view = self.arena[0:p, base:base + f32c]
        if dt == BF16:
            view = self.arena[0:128, base:base + f32c].bitcast(BF16)[0:p, 0:cols]
        return view

    def ps(self, shape, dt=F32):
        p, cols = shape
        bytes_pp = cols * (4 if dt == F32 else 2)
        nb = (bytes_pp + 2047) // 2048
        base = self.pcur
        self.pcur += nb
        assert self.pcur <= 8, "psum overflow"
        if dt == F32:
            return self.psum[0:p, base * 512: base * 512 + cols]
        return self.psum[0:128, base * 512: base * 512 + (cols + 1) // 2].bitcast(BF16)[0:p, 0:cols]

    def sem(self, name):
        self.semn += 1
        return Sem(self.es, self.nc, f"s{self.semn}_{name}")

    def stage_begin(self):
        idx = self.stage
        for e in ENGINES:
            @on(self.q, e)
            def _(eng, idx=idx):
                eng.wait_ge(self.s_bar.h, idx)

    def stage_end(self, waits):
        self.stage += 1

        @on(self.q, "sync")
        def _(s, waits=list(waits), me=self.stage):
            for sem, n in waits:
                if n:
                    s.wait_ge(sem.h, n)
            s.sem_inc(self.s_bar.h, 1)



def emit_pre(C, io):
    nc, q = C.nc, C.q
    x_ext = io["x_ext"]; w_in = io["w_in"]; cw_in = io["cw_in"]; cb_in = io["cb_in"]
    w_xp_in = io["w_xp"]; w_dtp_in = io["w_dtp"]; dtb_in = io["dtb_in"]
    ident_in = io["ident_in"]; cv_in = io["cv_in"]
    pre_f32 = io["pre_f32_d"]
    pre_bf = io["pre_bf_d"]
    xn_d = io["xn_d"]
    z_d = io["z_d"]

    sb, ps, sem = C.sb, C.ps, C.sem
    xt = [sb([128, D], F32) for _ in range(2)]
    sq = sb([128, D], F32)
    st1 = sb([128, 1], F32); st2 = sb([128, 1], F32); stm = sb([128, 1], F32)
    stv = sb([128, 1], F32); str_ = sb([128, 1], F32)
    xn_tm = sb([128, D], BF16)
    identf = io["identf"]; ident = io["identb"]; cv = io["cv"]
    xn_fm = sb([128, 16 * TH], BF16)
    wbuf = [sb([128, 256], BF16) for _ in range(2)]
    xconv = [sb([128, TH], F32) for _ in range(2)]
    zstage = [sb([128, T], BF16) for _ in range(2)]
    xastage = [sb([128, T], BF16) for _ in range(2)]
    cacc = [sb([128, T], F32) for _ in range(2)]
    cact = [sb([128, T], F32) for _ in range(2)]
    cw = sb([128, 32 * 4], F32)
    cb = sb([128, 32], F32)
    dtb = sb([128, 32], F32)
    dtb2 = sb([128, 32], F32)
    xact_bf = sb([128, 32 * T], BF16)
    wxp = sb([128, 32 * 160], BF16)
    wdtp = sb([128, DI], BF16)
    dtr_bf = sb([128, T], BF16)
    bc_sb = sb([32, T], F32)
    spv = [sb([128, T], F32) for _ in range(2)]
    spt = [sb([128, T], F32) for _ in range(2)]
    spr = [sb([128, T], F32) for _ in range(2)]

    pb = [ps([128, 512]) for _ in range(4)]
    ph = [ps([128, 48]) for _ in range(2)]
    pt = [ps([128, 128], BF16)]

    s_xs = [sem("pxs0"), sem("pxs1")]
    s_ws = [sem("pws0"), sem("pws1")]
    s_wp = sem("pwp")
    s_stz = [sem("pstz0"), sem("pstz1")]
    s_sta = [sem("psta0"), sem("psta1")]
    s_std = [sem("pstd0"), sem("pstd1")]
    s_c = sem("pc")
    s_wf = sem("pwf")
    s_mm = sem("pmm")
    s_vv = sem("pvv")
    s_aa = sem("paa")
    s_ev = sem("pev")
    s_st = sem("pst")

    C.stage_begin()

    xrows = [(i * 128, min(128, TH - i * 128)) for i in range((TH + 127) // 128)]
    NXT = len(xrows)

    @on(q, "sync")
    def _(s):
        s.dma_start(cw[:], cw_in).then_inc(s_c.h, 16)
        s.dma_start(cb[:], cb_in).then_inc(s_c.h, 16)
        s.dma_start(dtb[:], dtb_in).then_inc(s_c.h, 16)
    s_c.bump(48)

    @on(q, "gpsimd")
    def _(g):
        g.dma_start(wxp[:].rearrange("p (kt m) -> p kt m", kt=32),
                    w_xp_in.rearrange("(kt p) m -> p kt m", p=128)).then_inc(s_wp.h, 16)
        g.dma_start(wdtp[:], w_dtp_in).then_inc(s_wp.h, 16)
    s_wp.bump(32)
    W_PRELOAD = s_wp.n

    cm = s_c.n

    @on(q, "vector")
    def _(v, cm=cm, gc=io["gconst_vv"]):
        v.wait_ge(s_c.h, cm)
        v.wait_ge(io["s_gc"].h, gc)
        v.tensor_scalar_mul(dtb2[:], dtb[:], 2.0).then_inc(s_vv.h, 1)
    s_vv.bump(1)
    CONST_VV = s_vv.n

    xload_mark = []
    for i, (r0, nr) in enumerate(xrows[:2]):
        @on(q, "sync")
        def _(s, i=i, r0=r0, nr=nr):
            s.dma_start(xt[i % 2][0:nr, :], x_ext[r0:r0 + nr, :]).then_inc(s_xs[i % 2].h, 16)
        xload_mark.append((i % 2, s_xs[i % 2].bump(16)))

    xn_ready = []
    tile_done = []
    for i, (r0, nr) in enumerate(xrows):
        xm = xload_mark[i]
        xv_ = xt[i % 2]
        am = s_aa.bump()

        @on(q, "scalar")
        def _(s, xm=xm, nr=nr, xv_=xv_, i=i, pv=(xn_ready[i - 1] if i >= 1 else 0)):
            if i == 0:
                s.wait_ge(s_c.h, 48)
                s.wait_ge(io["s_gc"].h, io["gconst_dma"])
            s.wait_ge(s_xs[xm[0]].h, xm[1])
            if pv:
                s.wait_ge(s_vv.h, pv)
            s.activation(sq[0:nr, :], xv_[0:nr, :], AF.Square).then_inc(s_aa.h, 1)

        m_s1 = s_vv.bump(); m_s2 = s_vv.bump(); m_m = s_vv.bump()
        m_msq = s_vv.bump(); m_var = s_vv.bump()

        @on(q, "vector")
        def _(v, xm=xm, nr=nr, xv_=xv_, am=am, m_s1=m_s1, m_m=m_m, m_msq=m_msq):
            v.wait_ge(s_xs[xm[0]].h, xm[1])
            v.reduce_sum(st1[0:nr], xv_[0:nr, :], axis=AX.X).then_inc(s_vv.h, 1)
            v.wait_ge(s_aa.h, am)
            v.reduce_sum(st2[0:nr], sq[0:nr, :], axis=AX.X).then_inc(s_vv.h, 1)
            v.wait_ge(s_vv.h, m_s1)
            v.tensor_scalar_mul(stm[0:nr], st1[0:nr], 1.0 / D).then_inc(s_vv.h, 1)
            v.wait_ge(s_vv.h, m_m)
            v.tensor_mul(stv[0:nr], stm[0:nr], stm[0:nr]).then_inc(s_vv.h, 1)
            v.wait_ge(s_vv.h, m_msq)
            v.scalar_tensor_tensor(stv[0:nr], st2[0:nr], 1.0 / D, stv[0:nr],
                                   ALU.mult, ALU.subtract).then_inc(s_vv.h, 1)

        am2 = s_aa.bump()

        @on(q, "scalar")
        def _(s, nr=nr, m_var=m_var):
            s.wait_ge(s_vv.h, m_var)
            s.activation(str_[0:nr], stv[0:nr], AF.Sqrt, bias=cv[0:nr, 0:1]).then_inc(s_aa.h, 1)

        m_rec = s_vv.bump(); m_xn = s_vv.bump()
        xn_ready.append(m_xn)

        @on(q, "vector")
        def _(v, nr=nr, xv_=xv_, am2=am2, m_rec=m_rec,
              td=(tile_done[i - 1] if i >= 1 else 0)):
            v.wait_ge(s_aa.h, am2)
            v.reciprocal(str_[0:nr], str_[0:nr]).then_inc(s_vv.h, 1)
            if td:
                v.wait_ge(s_ev.h, td)
            v.wait_ge(s_vv.h, m_rec)
            v.tensor_scalar(xn_tm[0:nr, :], xv_[0:nr, :], stm[0:nr], str_[0:nr],
                            ALU.subtract, ALU.mult).then_inc(s_vv.h, 1)

        if i + 2 < NXT:
            r2, nr2 = xrows[i + 2]

            @on(q, "sync")
            def _(s, i=i, r2=r2, nr2=nr2, m_xn=m_xn):
                s.wait_ge(s_vv.h, m_xn)
                s.dma_start(xt[i % 2][0:nr2, :], x_ext[r2:r2 + nr2, :]).then_inc(s_xs[i % 2].h, 16)
            xload_mark.append((i % 2, s_xs[i % 2].bump(16)))

        ev_base = s_ev.n
        for kt in range(16):
            mmark = s_mm.bump()

            @on(q, "tensor")
            def _(t, kt=kt, nr=nr, m_xn=m_xn, i=i,
                  ev_need=(ev_base + kt if kt >= 1 else (tile_done[i - 1] if i >= 1 else 0))):
                if kt == 0:
                    t.wait_ge(s_vv.h, max(m_xn, CONST_VV))
                if ev_need:
                    t.wait_ge(s_ev.h, ev_need)
                t.transpose(pt[0][:, 0:nr], xn_tm[0:nr, 128 * kt:128 * (kt + 1)],
                            ident[0:nr, 0:nr]).then_inc(s_mm.h, 1)

            s_ev.bump()

            @on(q, "vector")
            def _(v, kt=kt, nr=nr, i=i, mmark=mmark):
                v.wait_ge(s_mm.h, mmark)
                col = kt * TH + i * 128
                v.tensor_copy(xn_fm[:, col:col + nr], pt[0][:, 0:nr]).then_inc(s_ev.h, 1)
        tile_done.append(s_ev.n)

    LN_EV = s_ev.n

    @on(q, "sync")
    def _(s):
        s.wait_ge(s_ev.h, LN_EV)
        for kt in range(16):
            s.dma_start(xn_d[128 * kt:128 * (kt + 1), :],
                        xn_fm[:, kt * TH + 3: kt * TH + 3 + T]).then_inc(s_st.h, 16)
    s_st.bump(16 * 16)

    NB, NK = 32, 16
    ev_blk = {}
    conv_vv = {}
    zst_mark = {}
    xa_mark = {}
    silu_mark = {}

    for b in range(NB):
        is_x = b < 16
        for k in range(NK):
            gi = b * NK + k
            wfm_need = gi - 1
            slot = gi % 2
            wm = s_ws[slot].bump(16)

            @on(q, "gpsimd")
            def _(g, b=b, k=k, gi=gi, wfm_need=wfm_need, slot=slot):
                if wfm_need > 0:
                    g.wait_ge(s_wf.h, wfm_need)
                g.dma_start(wbuf[slot][:], w_in[128 * k:128 * (k + 1),
                                                256 * b:256 * (b + 1)]).then_inc(s_ws[slot].h, 16)

            wfm = s_wf.bump()

            @on(q, "tensor")
            def _(t, b=b, k=k, gi=gi, wm=wm, is_x=is_x, slot=slot,
                  evn=(ev_blk.get(b - 1, 0) if k == 0 else 0)):
                t.wait_ge(s_ws[slot].h, wm)
                if b == 0 and k == 0:
                    t.wait_ge(s_ev.h, LN_EV)
                if k == 0 and evn:
                    t.wait_ge(s_ev.h, evn)
                first = k == 0
                last = k == NK - 1
                lastmm = None
                for j in range(2):
                    lhs = wbuf[slot][:, 128 * j:128 * (j + 1)]
                    for ci in range(2):
                        rhs = xn_fm[:, k * TH + 3 + 512 * ci: k * TH + 3 + 512 * (ci + 1)]
                        lastmm = t.matmul(pb[2 * j + ci][:], lhs, rhs, start=first, stop=last)
                    if is_x:
                        rhs_h = xn_fm[:, k * TH: k * TH + 3]
                        lastmm = t.matmul(ph[j][:, 3 * b: 3 * b + 3], lhs, rhs_h,
                                          start=first, stop=last)
                lastmm.then_inc(s_wf.h, 1)

        blk_wf = s_wf.n

        if is_x:
            for j in range(2):
                et = 2 * b + j
                xcb = xconv[j]
                prev_conv = conv_vv.get(2 * (b - 1) + j, 0)
                e1 = s_ev.bump(3)

                @on(q, "vector")
                def _(v, j=j, et=et, xcb=xcb, blk_wf=blk_wf, prev_conv=prev_conv):
                    v.wait_ge(s_wf.h, blk_wf)
                    if prev_conv:
                        v.wait_ge(s_vv.h, prev_conv)
                    v.tensor_copy(xcb[:, 0:3], ph[j][:, 3 * (et // 2): 3 * (et // 2) + 3]).then_inc(s_ev.h, 1)
                    v.tensor_copy(xcb[:, 3:515], pb[2 * j + 0][:]).then_inc(s_ev.h, 1)
                    v.tensor_copy(xcb[:, 515:1027], pb[2 * j + 1][:]).then_inc(s_ev.h, 1)

                prev_silu = silu_mark.get(et - 2, 0)
                c0 = s_vv.bump(); c1 = s_vv.bump(); c2 = s_vv.bump(); c3 = s_vv.bump()

                @on(q, "vector")
                def _(v, j=j, et=et, xcb=xcb, e1=e1, prev_silu=prev_silu,
                      c0=c0, c1=c1, c2=c2):
                    v.wait_ge(s_ev.h, e1)
                    if prev_silu:
                        v.wait_ge(s_aa.h, prev_silu)
                    v.tensor_scalar_mul(cacc[j][:], xcb[:, 0:T],
                                        cw[:, 4 * et: 4 * et + 1]).then_inc(s_vv.h, 1)
                    for kk, pm in ((1, c0), (2, c1), (3, c2)):
                        v.wait_ge(s_vv.h, pm)
                        v.scalar_tensor_tensor(cacc[j][:], xcb[:, kk:kk + T],
                                               cw[:, 4 * et + kk: 4 * et + kk + 1],
                                               cacc[j][:], ALU.mult, ALU.add).then_inc(s_vv.h, 1)
                conv_mark = c3
                conv_vv[et] = conv_mark

                sil = s_aa.bump()
                silu_mark[et] = sil

                @on(q, "scalar")
                def _(s, j=j, et=et, conv_mark=conv_mark,
                      prev_cast=xa_mark.get(et - 2, 0)):
                    s.wait_ge(s_vv.h, conv_mark)
                    if prev_cast:
                        s.wait_ge(s_vv.h, prev_cast)
                    s.activation(cact[j][:], cacc[j][:], AF.Silu,
                                 bias=cb[:, et:et + 1]).then_inc(s_aa.h, 1)

                xam = s_vv.bump(2)
                xa_mark[et] = xam

                @on(q, "vector")
                def _(v, j=j, et=et, sil=sil, prev_st=zst_mark.get(("xa", et - 2), 0)):
                    v.wait_ge(s_aa.h, sil)
                    v.tensor_copy(xact_bf[:, et * T:(et + 1) * T], cact[j][:]).then_inc(s_vv.h, 1)
                    if prev_st:
                        v.wait_ge(s_sta[j].h, prev_st)
                    v.tensor_copy(xastage[j][:], cact[j][:]).then_inc(s_vv.h, 1)

                stm_ = s_sta[j].bump(16)
                zst_mark[("xa", et)] = stm_

                @on(q, "sync")
                def _(s, j=j, et=et, xam=xam):
                    s.wait_ge(s_vv.h, xam)
                    s.dma_start(pre_bf[128 * et:128 * (et + 1), :], xastage[j][:]).then_inc(s_sta[j].h, 16)
            ev_blk[b] = s_ev.n
        else:
            for j in range(2):
                et = 2 * b + j
                ez = et - 32
                prev_st = zst_mark.get(("z", ez - 2), 0)
                em = s_ev.bump(2)

                @on(q, "vector")
                def _(v, j=j, blk_wf=blk_wf, prev_st=prev_st):
                    v.wait_ge(s_wf.h, blk_wf)
                    if prev_st:
                        v.wait_ge(s_stz[j].h, prev_st)
                    v.tensor_copy(zstage[j][:, 0:512], pb[2 * j + 0][:]).then_inc(s_ev.h, 1)
                    v.tensor_copy(zstage[j][:, 512:1024], pb[2 * j + 1][:]).then_inc(s_ev.h, 1)

                stm_ = s_stz[j].bump(16)
                zst_mark[("z", ez)] = stm_

                @on(q, "sync")
                def _(s, j=j, ez=ez, em=em):
                    s.wait_ge(s_ev.h, em)
                    s.dma_start(z_d[128 * ez:128 * (ez + 1), :], zstage[j][:]).then_inc(s_stz[j].h, 16)
            ev_blk[b] = s_ev.n

    last_conv_cast = max(xa_mark.values())
    xp_mm = s_mm.bump()

    @on(q, "tensor")
    def _(t, last_conv_cast=last_conv_cast):
        t.wait_ge(s_wp.h, W_PRELOAD)
        t.wait_ge(s_vv.h, last_conv_cast)
        t.wait_ge(s_ev.h, ev_blk[NB - 1])
        for ci in range(2):
            for kt in range(32):
                first = kt == 0
                last = kt == 31
                lhsA = wxp[:, kt * 160: kt * 160 + 128]
                lhsB = wxp[:, kt * 160 + 128: kt * 160 + 160]
                rhs = xact_bf[:, kt * T + 512 * ci: kt * T + 512 * (ci + 1)]
                t.matmul(pb[ci][:], lhsA, rhs, start=first, stop=last)
                mmB = t.matmul(pb[2 + ci][0:32, :], lhsB, rhs, start=first, stop=last)
        mmB.then_inc(s_mm.h, 1)

    xp_ev = s_ev.bump(4)

    @on(q, "vector")
    def _(v, xp_mm=xp_mm):
        v.wait_ge(s_mm.h, xp_mm)
        v.tensor_copy(dtr_bf[:, 0:512], pb[0][:]).then_inc(s_ev.h, 1)
        v.tensor_copy(dtr_bf[:, 512:1024], pb[1][:]).then_inc(s_ev.h, 1)
        v.tensor_copy(bc_sb[:, 0:512], pb[2][0:32, :]).then_inc(s_ev.h, 1)
        v.tensor_copy(bc_sb[:, 512:1024], pb[3][0:32, :]).then_inc(s_ev.h, 1)

    bc_st = s_st.bump(16)

    @on(q, "sync")
    def _(s, xp_ev=xp_ev):
        s.wait_ge(s_ev.h, xp_ev)
        s.dma_start(pre_f32[DI:DI + 32, :], bc_sb[:]).then_inc(s_st.h, 16)

    sp_done = {}
    sp_vv_mark = {}
    for mt in range(32):
        pbi = [pb[2 * (mt % 2)], pb[2 * (mt % 2) + 1]]
        mmark = s_mm.bump()

        @on(q, "tensor")
        def _(t, mt=mt, pbi=pbi, xp_ev=xp_ev, mmark=mmark, ev_need=sp_done.get(mt - 2, 0)):
            if mt in (0, 1):
                t.wait_ge(s_ev.h, xp_ev)
            if ev_need:
                t.wait_ge(s_aa.h, ev_need)
            lhs = wdtp[:, 128 * mt:128 * (mt + 1)]
            t.matmul(pbi[0][:], lhs, dtr_bf[:, 0:512], start=True, stop=True)
            t.matmul(pbi[1][:], lhs, dtr_bf[:, 512:1024], start=True, stop=True).then_inc(s_mm.h, 1)

        ja = mt % 2
        a1_ = s_aa.bump(4)
        sp_done[mt] = a1_
        a2_ = s_aa.bump(2)

        @on(q, "scalar")
        def _(s, mt=mt, pbi=pbi, ja=ja, mmark=mmark,
              prev_st=zst_mark.get(("dt", mt - 2), 0),
              prev_vv=sp_vv_mark.get(mt - 2, 0), a1_=a1_):
            s.wait_ge(s_mm.h, mmark)
            if prev_st:
                s.wait_ge(s_std[ja].h, prev_st)
            if prev_vv:
                s.wait_ge(s_vv.h, prev_vv)
            b_ = dtb2[:, mt:mt + 1]
            s.activation(spt[ja][:, 0:512], pbi[0][:], AF.Abs, bias=b_).then_inc(s_aa.h, 1)
            s.activation(spt[ja][:, 512:1024], pbi[1][:], AF.Abs, bias=b_).then_inc(s_aa.h, 1)
            s.activation(spr[ja][:, 0:512], pbi[0][:], AF.Relu, bias=b_).then_inc(s_aa.h, 1)
            s.activation(spr[ja][:, 512:1024], pbi[1][:], AF.Relu, bias=b_).then_inc(s_aa.h, 1)
            s.wait_ge(s_aa.h, a1_ - 2)
            s.activation(spt[ja][:], spt[ja][:], AF.Exp, scale=-1.0).then_inc(s_aa.h, 1)
            s.wait_ge(s_aa.h, a1_ + 1)
            s.activation(spt[ja][:], spt[ja][:], AF.Ln, bias=1.0).then_inc(s_aa.h, 1)

        m_add = s_vv.bump()
        sp_vv_mark[mt] = m_add

        @on(q, "vector")
        def _(v, ja=ja, a2_=a2_):
            v.wait_ge(s_aa.h, a2_)
            v.tensor_add(spv[ja][:], spr[ja][:], spt[ja][:]).then_inc(s_vv.h, 1)

        stm_ = s_std[ja].bump(16)
        zst_mark[("dt", mt)] = stm_

        @on(q, "sync")
        def _(s, ja=ja, mt=mt, m_add=m_add):
            s.wait_ge(s_vv.h, m_add)
            s.dma_start(pre_f32[128 * mt:128 * (mt + 1), :], spv[ja][:]).then_inc(s_std[ja].h, 16)

    waits = [(s_st, s_st.n)] + [(x, x.n) for x in s_stz + s_sta + s_std]
    return waits


def emit_scan(C, io):
    nc, q = C.nc, C.q
    ag_f32 = io["ag_pre_f32_d"]
    ag_bf = io["ag_pre_bf_d"]
    alog_in = io["alog_in"]
    d_in = io["d_in"]
    y_d = io["y_d"]

    sb, ps, sem = C.sb, C.ps, C.sem
    dtt = [sb([128, NDT * TTS], F32) for _ in range(2)]
    xst = [sb([128, NDT * TTS], BF16) for _ in range(2)]
    bbc = [sb([128, DS * TTS], F32) for _ in range(2)]
    cbc = [sb([128, DS * TTS], F32) for _ in range(2)]
    a_buf = [ps([128, TTS]) for _ in range(2)]
    b_buf = [sb([128, TTS], F32) for _ in range(2)]
    hbuf = sb([128, DS * TTS], F32)
    hstate = sb([128, NDT * DS], F32)
    u_buf = ps([128, TTS])
    ycol = sb([128, TTS], F32)
    y_st = [sb([128, TTS], BF16) for _ in range(2)]
    alog = sb([128, NDT * DS], F32)
    amat = sb([128, NDT * DS], F32)
    dmat = sb([128, NDT], F32)

    s_ld = [sem("sld0"), sem("sld1")]
    s_lx = [sem("slx0"), sem("slx1")]
    s_bc = sem("sbc")
    s_cn = sem("scn")
    s_aa = sem("saa")
    s_vv = sem("svv")
    s_gp = sem("sgp")
    s_sty = [sem("ssty0"), sem("ssty1")]

    C.stage_begin()

    @on(q, "sync")
    def _(s):
        s.dma_start(alog[:], alog_in).then_inc(s_cn.h, 16)
        s.dma_start(dmat[:], d_in).then_inc(s_cn.h, 16)
    s_cn.bump(32)

    am0 = s_aa.bump()

    @on(q, "scalar")
    def _(s, n=s_cn.n):
        s.wait_ge(s_cn.h, n)
        s.activation(amat[:], alog[:], AF.Exp).then_inc(s_aa.h, 1)

    am1 = s_vv.bump()

    @on(q, "vector")
    def _(v, am0=am0):
        v.wait_ge(s_aa.h, am0)
        v.tensor_scalar_mul(amat[:], amat[:], -1.0).then_inc(s_vv.h, 1)
    A_READY = am1

    DYN = io["DYN"]

    def emit_dtx_load(tt2, guard_vv, guard_gp=0):
        slot = tt2 % 2
        r = tt2 // 2
        hc = (tt2 % 2) * TTS
        mk = s_ld[slot].bump(16)
        mkx = s_lx[slot].bump(16)

        @on(q, "sync")
        def _(s, r=r, hc=hc, slot=slot, guard_vv=guard_vv, guard_gp=guard_gp):
            if guard_vv:
                s.wait_ge(s_vv.h, guard_vv)
            if guard_gp:
                s.wait_ge(s_gp.h, guard_gp)
            src_dt = DYN(s, ag_f32, (r * PRE_ROWS) * T + hc, CPC * T,
                         [[T, 128], [128 * T, NDT], [1, TTS]])
            s.dma_start(dtt[slot][:].rearrange("p (n t) -> p n t", n=NDT), src_dt).then_inc(s_ld[slot].h, 16)

        @on(q, "gpsimd")
        def _(g, r=r, hc=hc, slot=slot, guard_vv=guard_vv, guard_gp=guard_gp):
            if guard_vv:
                g.wait_ge(s_vv.h, guard_vv)
            src_xs = DYN(g, ag_bf, (r * DI) * T + hc, CPC * T,
                         [[T, 128], [128 * T, NDT], [1, TTS]])
            g.dma_start(xst[slot][:].rearrange("p (n t) -> p n t", n=NDT), src_xs).then_inc(s_lx[slot].h, 16)
        return (mk, mkx)

    def emit_bc_load(tt, guard_vv):
        mk = s_bc.bump(32)

        @on(q, "gpsimd")
        def _(g, tt=tt, guard_vv=guard_vv):
            if guard_vv:
                g.wait_ge(s_vv.h, guard_vv)
            r = tt // 2
            hc = (tt % 2) * TTS
            base = (r * PRE_ROWS + DI) * T + hc
            srcb = bass.AP(ag_f32, base, [[0, 128], [T, DS], [1, TTS]])
            srcc = bass.AP(ag_f32, base + DS * T, [[0, 128], [T, DS], [1, TTS]])
            g.dma_start(bbc[tt % 2][:], srcb).then_inc(s_bc.h, 16)
            g.dma_start(cbc[tt % 2][:], srcc).then_inc(s_bc.h, 16)
        return mk

    ld_marks = {0: emit_dtx_load(0, 0), 1: emit_dtx_load(1, 0)}
    bc_marks = {0: emit_bc_load(0, 0)}

    scan_last = 0
    ucons_last = {}
    hfree_mark = 0
    st_copy_mark = {}
    aa_scan_read = {}
    ycol_read = [0]

    gacc_prev = {}
    pending = [None]

    def flush_pending():
        if pending[0] is None:
            return
        ftt, fdtile, fxss, fgp = pending[0]
        pending[0] = None
        styg = s_sty[ftt % 2].n
        myd = s_vv.bump(); myc = s_vv.bump()
        ycol_read[0] = myc

        @on(q, "vector")
        def _(v, fdtile=fdtile, fxss=fxss, ftt=ftt, fgp=fgp, myd=myd, styg=styg):
            v.wait_ge(s_gp.h, fgp)
            v.scalar_tensor_tensor(ycol[:], fxss[:, :], dmat[:, fdtile:fdtile + 1],
                                   ycol[:], ALU.mult, ALU.add).then_inc(s_vv.h, 1)
            v.wait_ge(s_vv.h, myd)
            if styg:
                v.wait_ge(s_sty[ftt % 2].h, styg)
            v.tensor_copy(y_st[ftt % 2][:], ycol[:]).then_inc(s_vv.h, 1)

        s_sty[ftt % 2].bump(16)

        @on(q, "sync")
        def _(s, fdtile=fdtile, ftt=ftt, myc=myc):
            s.wait_ge(s_vv.h, myc)
            s.dma_start(y_d[128 * fdtile:128 * (fdtile + 1), TTS * ftt:TTS * (ftt + 1)],
                        y_st[ftt % 2][:]).then_inc(s_sty[ftt % 2].h, 16)

    for tt in range(NTT):
        slot = tt % 2
        ldm, lxm = ld_marks[tt]
        bcm = bc_marks[tt]
        if tt + 1 < NTT and (tt + 1) not in bc_marks:
            bc_marks[tt + 1] = emit_bc_load(tt + 1, s_vv.n)
        for dtile in range(NDT):
            dts = dtt[slot][:, dtile * TTS:(dtile + 1) * TTS]
            xss = xst[slot][:, dtile * TTS:(dtile + 1) * TTS]
            mu = s_vv.bump()

            @on(q, "vector")
            def _(v, dts=dts, xss=xss, ldm=ldm, lxm=lxm, slot=slot):
                v.wait_ge(s_ld[slot].h, ldm)
                v.wait_ge(s_lx[slot].h, lxm)
                v.tensor_mul(u_buf[:], dts[:, :], xss[:, :]).then_inc(s_vv.h, 1)

            scm = {}
            for si in range(DS):
                asl = si % 2
                col = dtile * DS + si
                ma = s_aa.bump()

                @on(q, "scalar")
                def _(s, dts=dts, col=col, asl=asl, ldm=ldm, slot=slot,
                      prev_rd=aa_scan_read.get(asl, 0), tt=tt):
                    if tt == 0 and col == 0:
                        s.wait_ge(s_vv.h, A_READY)
                    s.wait_ge(s_ld[slot].h, ldm)
                    if prev_rd:
                        s.wait_ge(s_vv.h, prev_rd)
                    s.activation(a_buf[asl][:], dts[:, :], AF.Exp,
                                 scale=amat[:, col:col + 1]).then_inc(s_aa.h, 1)

                mb = s_vv.bump()

                @on(q, "vector")
                def _(v, si=si, asl=asl, mu=mu, bcm=bcm, bsl=tt % 2):
                    v.wait_ge(s_vv.h, mu)
                    v.wait_ge(s_bc.h, bcm)
                    v.tensor_mul(b_buf[asl][:], u_buf[:],
                                 bbc[bsl][:, si * TTS:(si + 1) * TTS]).then_inc(s_vv.h, 1)

                ms = s_vv.bump()
                mc = s_vv.bump()

                @on(q, "vector")
                def _(v, si=si, asl=asl, col=col, ma=ma, mb=mb, tt=tt, ms=ms,
                      hfree=gacc_prev.get(si, 0)):
                    v.wait_ge(s_aa.h, ma)
                    v.wait_ge(s_vv.h, mb)
                    if hfree:
                        v.wait_ge(s_gp.h, hfree)
                    init = 0.0 if tt == 0 else hstate[:, col:col + 1]
                    v.tensor_tensor_scan(hbuf[:, si * TTS:(si + 1) * TTS],
                                         a_buf[asl][:], b_buf[asl][:], init,
                                         ALU.mult, ALU.add).then_inc(s_vv.h, 1)
                    v.wait_ge(s_vv.h, ms)
                    v.tensor_copy(hstate[:, col:col + 1],
                                  hbuf[:, si * TTS + TTS - 1: si * TTS + TTS]).then_inc(s_vv.h, 1)
                scan_last = ms
                aa_scan_read[asl] = ms
                scm[si] = mc
            ucons_last[dtile] = scan_last

            flush_pending()
            if dtile == 0 and 2 <= tt + 1 < NTT:
                ld_marks[tt + 1] = emit_dtx_load(tt + 1, s_vv.n, s_gp.n)

            gm = {}
            for si in range(DS):
                gm[si] = (s_gp.bump(), s_gp.bump())

            @on(q, "gpsimd")
            def _(g, bsl=tt % 2, scm=dict(scm), gm=dict(gm), ycolg=ycol_read[0]):
                for si in range(DS):
                    g.wait_ge(s_vv.h, scm[si])
                    g.tensor_mul(hbuf[:, si * TTS:(si + 1) * TTS],
                                 hbuf[:, si * TTS:(si + 1) * TTS],
                                 cbc[bsl][:, si * TTS:(si + 1) * TTS]).then_inc(s_gp.h, 1)
                    g.wait_ge(s_gp.h, gm[si][0])
                    if si == 0:
                        if ycolg:
                            g.wait_ge(s_vv.h, ycolg)
                        g.tensor_copy(ycol[:], hbuf[:, 0:TTS]).then_inc(s_gp.h, 1)
                    else:
                        g.tensor_add(ycol[:], ycol[:],
                                     hbuf[:, si * TTS:(si + 1) * TTS]).then_inc(s_gp.h, 1)
            gacc_prev = {si: gm[si][1] for si in range(DS)}
            pending[0] = (tt, dtile, xss, gm[DS - 1][1])

    flush_pending()

    return [(x, x.n) for x in s_sty]



def emit_post0(C, io):
    nc, q = C.nc, C.q
    z_in = io["z_d"]; xn_in = io["xn_d"]; x_in = io["x_ext"][3:TH, :]
    ag_y = io["ag_y_d"]
    y_in = io["y_loc_h"].ap()
    w_op_in = io["w_op"]; wg_in = io["wg"]; wu_in = io["wu"]; wd_in = io["wd"]
    ga_in = io["ga"]; gb_in = io["gb"]; ua_in = io["ua"]; ub_in = io["ub"]
    da_in = io["da"]; db_in = io["db"]; wr_in = io["wr"]; tr_in = io["tr_in"]
    x1_d = io["x1_d"]; wt_d = io["wt_d"]
    identf = io["identf"]; cv = io["cv"]; DYN = io["DYN"]

    sb, ps, sem = C.sb, C.ps, C.sem
    big_bf = sb([128, 16 * T], BF16)
    xn_fm = sb([128, 16 * T], BF16)
    macc = sb([128, 16 * T], F32)
    ybuf = [sb([128, 512], BF16) for _ in range(2)]
    zbuf = [sb([128, 512], BF16) for _ in range(2)]
    wbuf = [sb([128, 16 * 128], BF16) for _ in range(2)]
    wbuf2 = [sb([128, 16 * 128], BF16) for _ in range(2)]
    wbig = [sb([128, 4 * 512], BF16) for _ in range(2)]
    ga_sb = sb([128, 16 * R0], BF16)
    ua_sb = sb([128, 16 * R0], BF16)
    da_sb = sb([128, 16 * R0], BF16)
    gb_sb = sb([R0, F0], BF16)
    ub_sb = sb([R0, F0], BF16)
    db_sb = sb([R0, D], BF16)
    wrsb = sb([128, 16 * E0], BF16)
    trsb = sb([128, 1], F32)
    rr = {nm: sb([128, E0], F32)
          for nm in ("inh", "z1", "z2", "dist", "w", "fb", "kn", "fbw")}
    rcol = {nm: sb([128, 1], F32)
            for nm in ("wsum", "noexp", "nos", "m1", "m2", "fbs", "fbr", "ws2")}
    wts = sb([128, 8 * E0], F32)
    wT = sb([E0, T], BF16)
    w_bc = [sb([128, T], F32) for _ in range(2)]
    tg_bf = sb([R0, T], BF16)
    tu_bf = sb([R0, T], BF16)
    td_bf = sb([R0, T], BF16)
    stmp = [sb([128, 512], F32) for _ in range(2)]
    mtmp = [sb([128, 512], F32) for _ in range(2)]
    xrow = [sb([128, 1024], F32)]
    x1row = [sb([128, 1024], F32)]

    pb = [ps([128, 512]) for _ in range(8)]

    s_cn = sem("qcn")
    s_cg = sem("qcg")
    s_yz = [sem("qyz0"), sem("qyz1")]
    s_ws = [sem("qws0"), sem("qws1")]
    s_wf = sem("qwf")
    s_mm = sem("qmm")
    s_vv = sem("qvv")
    s_aa = sem("qaa")
    s_ev = sem("qev")
    s_xl = [sem("qxl0"), sem("qxl1")]
    s_so = [sem("qso0"), sem("qso1")]

    s_ye = sem("qye")
    C.stage_begin()

    @on(q, "gpsimd")
    def _(g):
        src_y = DYN(g, ag_y, 0, T, [[S, 512], [CPC * S, 8], [1, T]])
        g.dma_start(y_in.rearrange("(r c) t -> c r t", r=8), src_y).then_inc(s_ye.h, 16)
    s_ye.bump(16)
    YE = s_ye.n

    gc = {"n": 0}

    @on(q, "sync")
    def _(s):
        s.dma_start(trsb[:], tr_in).then_inc(s_cn.h, 16)
        s.dma_start(xn_fm[:].rearrange("p (n t) -> p n t", n=16),
                    xn_in.rearrange("(n p) t -> p n t", p=128)).then_inc(s_cn.h, 16)

    @on(q, "gpsimd")
    def _(g):
        g.dma_start(wrsb[:].rearrange("p (n e) -> p n e", n=16),
                    wr_in.rearrange("(n p) e -> p n e", p=128)).then_inc(s_cg.h, 16)
    s_cn.bump(32)
    s_cg.bump(16)
    CN = s_cn.n
    CG = s_cg.n

    mcv = s_vv.bump(1)

    @on(q, "vector")
    def _(v, CN=CN, gc=io["gconst_vv"]):
        v.wait_ge(s_cn.h, CN)
        v.wait_ge(io["s_gc"].h, gc)
        v.tensor_scalar_mul(trsb[:], trsb[:], 1.0).then_inc(s_vv.h, 1)
    CONST_VV = mcv

    yzfree = {}
    op_ev_marks = {}
    for th in range(2):
        yz_done = []
        for kt in range(32):
            sl = kt % 2
            ym = s_yz[sl].bump(32)

            @on(q, "sync")
            def _(s, th=th, kt=kt, sl=sl, guard=yzfree.get((th * 32 + kt) - 2, 0)):
                if guard:
                    s.wait_ge(s_vv.h, guard)
                if th == 0 and kt == 0:
                    s.wait_ge(s_ye.h, YE)
                s.dma_start(ybuf[sl][:], y_in[128 * kt:128 * (kt + 1), 512 * th:512 * (th + 1)]).then_inc(s_yz[sl].h, 16)
                s.dma_start(zbuf[sl][:], z_in[128 * kt:128 * (kt + 1), 512 * th:512 * (th + 1)]).then_inc(s_yz[sl].h, 16)

            mz = s_vv.bump()
            yzfree[th * 32 + kt] = mz

            @on(q, "vector")
            def _(v, kt=kt, sl=sl, ym=ym, th=th, opk=(s_wf.n if th == 1 else 0)):
                v.wait_ge(s_yz[sl].h, ym)
                if th == 1 and kt == 0 and opk:
                    v.wait_ge(s_wf.h, opk)
                v.tensor_mul(big_bf[:, 512 * kt:512 * (kt + 1)], ybuf[sl][:], zbuf[sl][:]).then_inc(s_vv.h, 1)
            yz_done.append(mz)
        YZ_VV = yz_done[-1]

        for blk in range(4):
            bi = (th * 4 + blk)
            st_ = 4 * (bi % 2)
            ev_guard = op_ev_marks.get(bi - 2, 0)
            for kq in range(8):
                gi = gc["n"]; gc["n"] += 1
                slot = gi % 2
                need = s_wf.n - 1
                wm = s_ws[slot].bump(16)

                @on(q, "gpsimd")
                def _(g, blk=blk, kq=kq, slot=slot, need=need):
                    if need > 0:
                        g.wait_ge(s_wf.h, need)
                    g.dma_start(wbig[slot][:].rearrange("p (n m) -> p n m", n=4),
                                w_op_in.rearrange("(n p) m -> p n m", p=128)[:, 4 * kq:4 * (kq + 1),
                                                                            512 * blk:512 * (blk + 1)]).then_inc(s_ws[slot].h, 16)

                wfm = s_wf.bump()

                @on(q, "tensor")
                def _(t, blk=blk, kq=kq, slot=slot, wm=wm, st_=st_,
                      first=(kq == 0), last=(kq == 7),
                      yzv=YZ_VV, evg=ev_guard, th=th):
                    t.wait_ge(s_ws[slot].h, wm)
                    if first:
                        t.wait_ge(s_vv.h, yzv)
                        if evg:
                            t.wait_ge(s_ev.h, evg)
                    for k8 in range(4):
                        k = 4 * kq + k8
                        rhs = big_bf[:, 512 * k:512 * (k + 1)]
                        for j in range(4):
                            mm = t.matmul(pb[st_ + j][:], wbig[slot][:, k8 * 512 + 128 * j: k8 * 512 + 128 * (j + 1)],
                                          rhs, start=(first and k8 == 0), stop=(last and k8 == 3))
                    mm.then_inc(s_wf.h, 1)

            blk_wf = s_wf.n
            em = s_ev.bump(4)
            op_ev_marks[bi] = em

            @on(q, "vector")
            def _(v, blk=blk, th=th, blk_wf=blk_wf, st_=st_):
                v.wait_ge(s_wf.h, blk_wf)
                for j in range(4):
                    dtile = 4 * blk + j
                    v.tensor_copy(macc[:, dtile * T + 512 * th: dtile * T + 512 * th + 512],
                                  pb[st_ + j][:]).then_inc(s_ev.h, 1)
        OP_WF = s_wf.n
        OP_WF = s_wf.n
    OP_EV = s_ev.n

    rt_psfree = {}
    rt_vv = {}
    for i in range(8):
        bank = 4 + (i % 2)
        mmark = s_mm.bump()

        @on(q, "tensor")
        def _(t, i=i, bank=bank, mmark=mmark, cn=CN, cg=CG, prev=rt_psfree.get(i - 2, 0),
              opev=OP_EV):
            if i == 0:
                t.wait_ge(s_cn.h, cn)
                t.wait_ge(s_cg.h, cg)
                t.wait_ge(s_ev.h, opev)
            if prev:
                t.wait_ge(s_aa.h, prev)
            for kt in range(16):
                mm = t.matmul(pb[bank][0:128, 0:8],
                              xn_fm[:, kt * T + 128 * i: kt * T + 128 * (i + 1)],
                              wrsb[:, kt * E0:(kt + 1) * E0],
                              start=(kt == 0), stop=(kt == 15))
            mm.then_inc(s_mm.h, 1)

        a0 = s_aa.bump()
        rt_psfree[i] = a0

        @on(q, "scalar")
        def _(s, i=i, bank=bank, mmark=mmark, pv=rt_vv.get(i - 1, 0), cn=CN):
            if i == 0:
                s.wait_ge(s_cn.h, cn)
            s.wait_ge(s_mm.h, mmark)
            if pv:
                s.wait_ge(s_vv.h, pv)
            s.activation(rr["inh"][:, :], pb[bank][0:128, 0:8], AF.Sigmoid,
                         scale=trsb[:, 0:1]).then_inc(s_aa.h, 1)

        a1 = s_aa.bump(3)

        @on(q, "scalar")
        def _(s, a0=a0, a1=a1):
            s.wait_ge(s_aa.h, a0)
            s.activation(rr["dist"][:], rr["inh"][:], AF.Abs, bias=cv[:, 2:3]).then_inc(s_aa.h, 1)
            s.wait_ge(s_aa.h, a1 - 2)
            s.activation(rr["w"][:], rr["dist"][:], AF.Exp, scale=cv[:, 3:4]).then_inc(s_aa.h, 1)
            s.activation(rr["fb"][:], rr["dist"][:], AF.Exp, scale=cv[:, 4:5]).then_inc(s_aa.h, 1)

        nv = s_vv.bump(24)
        rt_vv[i] = nv

        @on(q, "vector")
        def _(v, i=i, a1=a1, base=nv - 24):
            k = [base]
            def stp(ins):
                ins.then_inc(s_vv.h, 1)
                k[0] += 1
                v.wait_ge(s_vv.h, k[0])
            v.wait_ge(s_aa.h, a1)
            stp(v.tensor_scalar(rr["z1"][:], rr["inh"][:], GOLDEN_LOWER, None, ALU.is_ge))
            stp(v.tensor_scalar(rr["z2"][:], rr["inh"][:], GOLDEN_UPPER, None, ALU.is_le))
            stp(v.tensor_mul(rr["z1"][:], rr["z1"][:], rr["z2"][:]))
            stp(v.tensor_mul(rr["w"][:], rr["w"][:], rr["z1"][:]))
            stp(v.reduce_sum(rcol["wsum"][:], rr["w"][:], axis=AX.X))
            stp(v.tensor_scalar(rcol["noexp"][:], rcol["wsum"][:], 1e-8, None, ALU.is_lt))
            stp(v.reduce_max(rcol["m1"][:], rr["fb"][:], axis=AX.X))
            stp(v.tensor_scalar(rr["kn"][:], rr["fb"][:], rcol["m1"][:], -1e30, ALU.is_ge, ALU.mult))
            stp(v.tensor_add(rr["kn"][:], rr["fb"][:], rr["kn"][:]))
            stp(v.reduce_max(rcol["m2"][:], rr["kn"][:], axis=AX.X))
            stp(v.tensor_scalar(rr["z2"][:], rr["fb"][:], rcol["m2"][:], None, ALU.is_ge))
            stp(v.tensor_mul(rr["fbw"][:], rr["fb"][:], rr["z2"][:]))
            stp(v.reduce_sum(rcol["fbs"][:], rr["fbw"][:], axis=AX.X))
            stp(v.tensor_scalar_max(rcol["fbs"][:], rcol["fbs"][:], 1e-8))
            stp(v.reciprocal(rcol["fbr"][:], rcol["fbs"][:]))
            stp(v.tensor_scalar_mul(rr["fbw"][:], rr["fbw"][:], rcol["fbr"][:]))
            stp(v.tensor_scalar(rcol["nos"][:], rcol["noexp"][:], -1.0, 1.0, ALU.mult, ALU.add))
            stp(v.tensor_scalar_mul(rr["w"][:], rr["w"][:], rcol["nos"][:]))
            stp(v.tensor_scalar_mul(rr["fbw"][:], rr["fbw"][:], rcol["noexp"][:]))
            stp(v.tensor_add(rr["w"][:], rr["w"][:], rr["fbw"][:]))
            stp(v.reduce_sum(rcol["ws2"][:], rr["w"][:], axis=AX.X))
            stp(v.tensor_scalar_max(rcol["ws2"][:], rcol["ws2"][:], 1e-8))
            stp(v.reciprocal(rcol["ws2"][:], rcol["ws2"][:]))
            stp(v.tensor_scalar_mul(wts[:, E0 * i:E0 * (i + 1)], rr["w"][:], rcol["ws2"][:]))

    WTS_VV = s_vv.n

    tr_ev0 = s_ev.n
    for i in range(8):
        mmark = s_mm.bump()

        @on(q, "tensor")
        def _(t, i=i, wv=WTS_VV, mmark=mmark, ev=(tr_ev0 + i if i >= 1 else 0),
              rtev=rt_psfree.get(7, 0)):
            if i == 0:
                t.wait_ge(s_vv.h, wv)
            if ev:
                t.wait_ge(s_ev.h, ev)
            t.transpose(pb[6][0:8, 0:128], wts[:, E0 * i:E0 * (i + 1)], identf[:]).then_inc(s_mm.h, 1)

        s_ev.bump()

        @on(q, "vector")
        def _(v, i=i, mmark=mmark):
            v.wait_ge(s_mm.h, mmark)
            v.tensor_copy(wT[:, 128 * i:128 * (i + 1)], pb[6][0:8, 0:128]).then_inc(s_ev.h, 1)
    WT_EV = s_ev.n
    s_wtd = C.sem("s_wtd")
    wtdm = s_wtd.bump(16)

    @on(q, "gpsimd")
    def _(g, ev=WT_EV):
        g.wait_ge(s_ev.h, ev)
        g.dma_start(wt_d.ap(), wT[:]).then_inc(s_wtd.h, 16)

    s_wbc = [C.sem("s_wbc0"), C.sem("s_wbc1")]

    wbc_used = {}
    prev_down_ev = OP_EV
    prev_up_free = ("ev", WT_EV)
    mid_done_mark = 0

    s_lr = C.sem("s_lr")
    lr_last_use = {"wf": 0}
    for e in range(E0):
        if e < NL:
            lrm = s_lr.bump(16 * 6)

            @on(q, "gpsimd")
            def _(g, e=e, need=lr_last_use["wf"]):
                if need:
                    g.wait_ge(s_wf.h, need)
                g.dma_start(ga_sb[:].rearrange("p (n r) -> p n r", n=16),
                            ga_in.rearrange("e (n p) r -> p e n r", p=128)[:, e]).then_inc(s_lr.h, 16)
                g.dma_start(ua_sb[:].rearrange("p (n r) -> p n r", n=16),
                            ua_in.rearrange("e (n p) r -> p e n r", p=128)[:, e]).then_inc(s_lr.h, 16)
                g.dma_start(da_sb[:].rearrange("p (n r) -> p n r", n=16),
                            da_in.rearrange("e (n p) r -> p e n r", p=128)[:, e]).then_inc(s_lr.h, 16)
                g.dma_start(gb_sb[:], gb_in.rearrange("e r f -> r e f")[:, e]).then_inc(s_lr.h, 16)
                g.dma_start(ub_sb[:], ub_in.rearrange("e r f -> r e f")[:, e]).then_inc(s_lr.h, 16)
                g.dma_start(db_sb[:], db_in.rearrange("e r d -> r e d")[:, e]).then_inc(s_lr.h, 16)
        bcw = s_wbc[e % 2].bump(16)

        @on(q, "gpsimd")
        def _(g, e=e, wtdm=wtdm, guard=wbc_used.get(e - 2, 0)):
            g.wait_ge(s_wtd.h, wtdm)
            if guard:
                g.wait_ge(s_vv.h, guard)
            g.dma_start(w_bc[e % 2][:],
                        bass.AP(wt_d, e * T, [[0, 128], [1, T]])).then_inc(s_wbc[e % 2].h, 16)

        lr_ev = WT_EV if e == 0 else 0
        if e < NL:
            for (asb, outb) in ((ga_sb, tg_bf), (ua_sb, tu_bf)):
                lm = s_mm.bump()

                @on(q, "tensor")
                def _(t, e=e, asb=asb, lm=lm, evg=lr_ev, lrm=s_lr.n, upg=prev_up_free):
                    t.wait_ge(s_lr.h, lrm)
                    if evg:
                        t.wait_ge(s_ev.h, evg)
                    if upg[1]:
                        t.wait_ge(s_ev.h if upg[0] == "ev" else s_vv.h, upg[1])
                    for tc in range(2):
                        for kt in range(16):
                            mm = t.matmul(pb[6 + tc][0:R0, :],
                                          asb[:, kt * R0:(kt + 1) * R0],
                                          xn_fm[:, kt * T + 512 * tc: kt * T + 512 * (tc + 1)],
                                          start=(kt == 0), stop=(kt == 15))
                    mm.then_inc(s_mm.h, 1)

                lr_ev = s_ev.bump(2)

                @on(q, "vector")
                def _(v, lm=lm, outb=outb):
                    v.wait_ge(s_mm.h, lm)
                    v.tensor_copy(outb[:, 0:512], pb[6][0:R0, :]).then_inc(s_ev.h, 1)
                    v.tensor_copy(outb[:, 512:1024], pb[7][0:R0, :]).then_inc(s_ev.h, 1)

        mid_evs = {}
        for blk in range(16):
            st_ = 4 * (blk % 2)
            if blk == 0:
                g0 = ("ev", prev_down_ev)
            elif blk == 1:
                g0 = prev_up_free
            else:
                g0 = ("vv", mid_evs[blk - 2])
            g4 = lr_ev if blk <= 1 else 0
            gi = gc["n"]; gc["n"] += 1
            slot = gi % 2
            need = s_wf.n - 1
            wm = s_ws[slot].bump(128)

            @on(q, "gpsimd")
            def _(g, e=e, blk=blk, slot=slot, need=need):
                if need > 0:
                    g.wait_ge(s_wf.h, need)
                for kq in range(4):
                    g.dma_start(wbuf[slot][:, 4 * 128 * kq: 4 * 128 * (kq + 1)].rearrange("p (n m) -> p n m", n=4),
                                bass.AP(wg_in.tensor, e * D * F0 + 128 * blk + 4 * kq * 128 * F0,
                                        [[F0, 128], [128 * F0, 4], [1, 128]])).then_inc(s_ws[slot].h, 16)
                    g.dma_start(wbuf2[slot][:, 4 * 128 * kq: 4 * 128 * (kq + 1)].rearrange("p (n m) -> p n m", n=4),
                                bass.AP(wu_in.tensor, e * D * F0 + 128 * blk + 4 * kq * 128 * F0,
                                        [[F0, 128], [128 * F0, 4], [1, 128]])).then_inc(s_ws[slot].h, 16)

            wfm = s_wf.bump()
            lowrank = e < NL

            @on(q, "tensor")
            def _(t, e=e, blk=blk, slot=slot, wm=wm, st_=st_, g0=g0, g4=g4, lowrank=lowrank):
                t.wait_ge(s_ws[slot].h, wm)
                if g0 and g0[1]:
                    t.wait_ge(s_ev.h if g0[0] == "ev" else s_vv.h, g0[1])
                if g4:
                    t.wait_ge(s_ev.h, g4)
                for k in range(16):
                    for tc in range(2):
                        rhs = xn_fm[:, k * T + 512 * tc: k * T + 512 * (tc + 1)]
                        t.matmul(pb[st_ + tc][:], wbuf[slot][:, 128 * k:128 * (k + 1)], rhs,
                                 start=(k == 0), stop=(k == 15 and not lowrank))
                        mm = t.matmul(pb[st_ + 2 + tc][:], wbuf2[slot][:, 128 * k:128 * (k + 1)], rhs,
                                      start=(k == 0), stop=(k == 15 and not lowrank))
                if lowrank:
                    for tc in range(2):
                        t.matmul(pb[st_ + tc][:], gb_sb[:, 128 * blk:128 * (blk + 1)],
                                 tg_bf[:, 512 * tc:512 * (tc + 1)], start=False, stop=True)
                        mm = t.matmul(pb[st_ + 2 + tc][:], ub_sb[:, 128 * blk:128 * (blk + 1)],
                                      tu_bf[:, 512 * tc:512 * (tc + 1)], start=False, stop=True)
                mm.then_inc(s_wf.h, 1)

            blk_wf = s_wf.n
            na = s_aa.bump(2)
            nv = s_vv.bump(4)
            mid_evs[blk] = nv
            bcev = bcw

            @on(q, "scalar")
            def _(s, e=e, blk=blk, blk_wf=blk_wf, st_=st_,
                  pv1=mid_evs.get(blk - 1, 0)):
                s.wait_ge(s_wf.h, blk_wf)
                for tc in range(2):
                    if pv1:
                        s.wait_ge(s_vv.h, pv1 - 3 + 2 * tc)
                    s.activation(stmp[tc][:], pb[st_ + tc][:], AF.Silu).then_inc(s_aa.h, 1)

            @on(q, "vector")
            def _(v, e=e, blk=blk, base_a=na - 2, base_v=nv - 4, bcev=bcev, st_=st_):
                v.wait_ge(s_wbc[e % 2].h, bcev)
                k = [base_v]
                for tc in range(2):
                    v.wait_ge(s_aa.h, base_a + tc + 1)
                    v.tensor_mul(mtmp[tc][:], stmp[tc][:], pb[st_ + 2 + tc][:]).then_inc(s_vv.h, 1)
                    k[0] += 1
                    v.wait_ge(s_vv.h, k[0])
                    v.tensor_mul(big_bf[:, blk * T + 512 * tc: blk * T + 512 * tc + 512],
                                 mtmp[tc][:], w_bc[e % 2][:, 512 * tc:512 * tc + 512]).then_inc(s_vv.h, 1)
                    k[0] += 1
            wbc_used[e] = nv
        MID_VV = wbc_used[e]

        dn_extra = None
        if e < NL:
            tdm = s_mm.bump()

            @on(q, "tensor")
            def _(t, e=e, tdm=tdm, mv=MID_VV):
                t.wait_ge(s_vv.h, mv)
                for tc in range(2):
                    for kt in range(16):
                        mm = t.matmul(pb[4 + tc][0:R0, :],
                                      da_sb[:, kt * R0:(kt + 1) * R0],
                                      big_bf[:, kt * T + 512 * tc: kt * T + 512 * (tc + 1)],
                                      start=(kt == 0), stop=(kt == 15))
                    mm.then_inc(s_mm.h, 1) if tc == 1 else None

            td_ev = s_ev.bump(2)

            @on(q, "vector")
            def _(v, tdm=tdm):
                v.wait_ge(s_mm.h, tdm)
                v.tensor_copy(td_bf[:, 0:512], pb[4][0:R0, :]).then_inc(s_ev.h, 1)
                v.tensor_copy(td_bf[:, 512:1024], pb[5][0:R0, :]).then_inc(s_ev.h, 1)
            dn_extra = td_ev

        down_evs = {}
        for blk in range(16):
            st_ = 2 * (blk % 2)
            gi = gc["n"]; gc["n"] += 1
            slot = gi % 2
            need = s_wf.n - 1
            wm = s_ws[slot].bump(64)

            @on(q, "gpsimd")
            def _(g, e=e, blk=blk, slot=slot, need=need):
                if need > 0:
                    g.wait_ge(s_wf.h, need)
                for kq in range(4):
                    g.dma_start(wbuf[slot][:, 4 * 128 * kq: 4 * 128 * (kq + 1)].rearrange("p (n m) -> p n m", n=4),
                                bass.AP(wd_in.tensor, e * F0 * D + 128 * blk + 4 * kq * 128 * D,
                                        [[D, 128], [128 * D, 4], [1, 128]])).then_inc(s_ws[slot].h, 16)

            wfm = s_wf.bump()
            lowrank = e < NL

            @on(q, "tensor")
            def _(t, e=e, blk=blk, slot=slot, wm=wm, st_=st_, lowrank=lowrank,
                  mv=MID_VV if blk == 0 else 0,
                  dg=(down_evs.get(blk - 2, 0)),
                  tdg=(dn_extra if (blk <= 1 and dn_extra) else 0)):
                t.wait_ge(s_ws[slot].h, wm)
                if mv:
                    t.wait_ge(s_vv.h, mv)
                if dg:
                    t.wait_ge(s_ev.h, dg)
                if tdg:
                    t.wait_ge(s_ev.h, tdg)
                for k in range(16):
                    for tc in range(2):
                        rhs = big_bf[:, k * T + 512 * tc: k * T + 512 * (tc + 1)]
                        mm = t.matmul(pb[st_ + tc][:], wbuf[slot][:, 128 * k:128 * (k + 1)], rhs,
                                      start=(k == 0), stop=(k == 15 and not lowrank))
                if lowrank:
                    for tc in range(2):
                        mm = t.matmul(pb[st_ + tc][:],
                                      db_sb[:, 128 * blk:128 * (blk + 1)],
                                      td_bf[:, 512 * tc:512 * (tc + 1)], start=False, stop=True)
                mm.then_inc(s_wf.h, 1)

            blk_wf = s_wf.n
            dn_ev = s_ev.bump(2)
            down_evs[blk] = dn_ev

            @on(q, "vector")
            def _(v, e=e, blk=blk, blk_wf=blk_wf, st_=st_):
                v.wait_ge(s_wf.h, blk_wf)
                for tc in range(2):
                    v.tensor_add(macc[:, blk * T + 512 * tc: blk * T + 512 * tc + 512],
                                 pb[st_ + tc][:],
                                 macc[:, blk * T + 512 * tc: blk * T + 512 * tc + 512]).then_inc(s_ev.h, 1)
        prev_down_ev = down_evs[15]
        prev_up_free = ("vv", MID_VV) if e >= NL else ("ev", td_ev)
        mid_done_mark = MID_VV
        if e < NL:
            lr_last_use["wf"] = s_wf.n

    FFN_EV = s_ev.n
    FFN_WF = s_wf.n

    prev_row_ev = 0
    for i in range(8):
        for hf in range(2):
            xm = s_xl[0].bump(16)

            @on(q, "sync")
            def _(s, i=i, hf=hf, xm=xm, sog=s_so[0].n, pre=prev_row_ev):
                if pre:
                    s.wait_ge(s_ev.h, pre)
                if sog:
                    s.wait_ge(s_so[0].h, sog)
                s.dma_start(xrow[0][:, 0:1024],
                            x_in[128 * i:128 * (i + 1), 1024 * hf:1024 * (hf + 1)]).then_inc(s_xl[0].h, 16)

            for k8 in range(8):
                kt = 8 * hf + k8
                mmark = s_mm.bump()

                @on(q, "tensor")
                def _(t, i=i, kt=kt, fev=FFN_EV, ev=(s_ev.n if (i or kt) else 0)):
                    if i == 0 and kt == 0:
                        t.wait_ge(s_ev.h, fev)
                    if ev:
                        t.wait_ge(s_ev.h, ev)
                    t.transpose(pb[7][0:128, 0:128],
                                macc[:, kt * T + 128 * i: kt * T + 128 * (i + 1)],
                                identf[:]).then_inc(s_mm.h, 1)

                s_ev.bump()

                @on(q, "vector")
                def _(v, i=i, k8=k8, xm=xm, mmark=mmark, sog=(s_so[0].n if k8 == 0 else 0)):
                    v.wait_ge(s_mm.h, mmark)
                    if k8 == 0:
                        v.wait_ge(s_xl[0].h, xm)
                        if sog:
                            v.wait_ge(s_so[0].h, sog)
                    v.tensor_add(x1row[0][:, 128 * k8:128 * (k8 + 1)],
                                 pb[7][0:128, 0:128],
                                 xrow[0][:, 128 * k8:128 * (k8 + 1)]).then_inc(s_ev.h, 1)
            prev_row_ev = s_ev.n

            @on(q, "sync")
            def _(s, i=i, hf=hf, row_ev=prev_row_ev):
                s.wait_ge(s_ev.h, row_ev)
                s.dma_start(x1_d[128 * i:128 * (i + 1), 1024 * hf:1024 * (hf + 1)],
                            x1row[0][:, 0:1024]).then_inc(s_so[0].h, 16)
            s_so[0].bump(16)

    return [(s_so[0], s_so[0].n), (s_so[1], s_so[1].n)]


def emit_l1a(C, io):
    nc, q = C.nc, C.q
    x1_in = io["x1_d"]
    wqd_in = io["wqd"]; wkd_in = io["wkd"]; wkr_in = io["wkr"]
    bqd_in = io["bqd"]; bkd_in = io["bkd"]; bkr_in = io["bkr"]
    gq_in = io["gq"]; gk_in = io["gk"]
    identf = io["identf"]; ident = io["identb"]; cv = io["cv"]; onesf = io["onesf"]
    xn1_d = io["xn1_d"]
    lat = io["lat_d"]
    lat_cq = lat[0:DCQ, :]
    lat_ckv = lat[DCQ:DCQ + DCKV, :]
    lat_kr = lat[DCQ + DCKV:2560, :]
    rms_d = io["rms_d"]

    sb, ps, sem = C.sb, C.ps, C.sem
    xt = [sb([128, D], F32) for _ in range(2)]
    sq = sb([128, D], F32)
    st1 = sb([128, 1], F32); st2 = sb([128, 1], F32); stm = sb([128, 1], F32)
    stv = sb([128, 1], F32); str_ = sb([128, 1], F32)
    dummy1 = sb([128, 1], F32)
    xn_tm = sb([128, D], BF16)
    xn_fm = sb([128, 16 * T], BF16)
    wbuf = [sb([128, 128], BF16) for _ in range(2)]
    craw = sb([128, 8 * T], F32)
    c2raw = sb([128, 4 * T], F32)
    sqt = [sb([128, 512], F32) for _ in range(2)]
    rrow = sb([1, T], F32)
    rbc = sb([128, T], F32)
    bqd = sb([128, 8], F32)
    bkd = sb([128, 4], F32)
    bkr = sb([128, 8], F32)
    gq = sb([128, 8], F32)
    gk = sb([128, 4], F32)
    cstage = [sb([128, T], BF16) for _ in range(2)]

    pb = [ps([128, 512]) for _ in range(4)]
    pr = ps([1, 1024])
    pt = ps([128, 128], BF16)

    s_x = [sem("ax0"), sem("ax1")]
    s_c = sem("ac")
    s_ws = [sem("aws0"), sem("aws1")]
    s_wf = sem("awf")
    s_mm = sem("amm")
    s_vv = sem("avv")
    s_aa = sem("aaa")
    s_ev = sem("aev")
    s_st = sem("ast")
    s_rd = sem("ard")
    s_rb = sem("arb")
    s_cs = [sem("acs0"), sem("acs1")]

    C.stage_begin()

    gcount = {"n": 0}

    @on(q, "sync")
    def _(s):
        s.dma_start(bqd[:], bqd_in).then_inc(s_c.h, 16)
        s.dma_start(bkd[:], bkd_in).then_inc(s_c.h, 16)
        s.dma_start(bkr[:], bkr_in).then_inc(s_c.h, 16)
        s.dma_start(gq[:], gq_in).then_inc(s_c.h, 16)
        s.dma_start(gk[:], gk_in).then_inc(s_c.h, 16)
    s_c.bump(80)
    CN = s_c.n

    mcv = s_vv.bump()

    @on(q, "vector")
    def _(v, CN=CN, gc=io["gconst_vv"]):
        v.wait_ge(s_c.h, CN)
        v.wait_ge(io["s_gc"].h, gc)
        v.tensor_scalar_mul(dummy1[:], onesf[:], 1.0).then_inc(s_vv.h, 1)
    CONST_VV = mcv

    xload = []
    for i in range(2):
        @on(q, "sync")
        def _(s, i=i):
            s.dma_start(xt[i][:], x1_in[128 * i:128 * (i + 1), :]).then_inc(s_x[i].h, 16)
        xload.append((i, s_x[i].bump(16)))

    xn_ready = []
    tile_done = []
    for i in range(8):
        sl, xm = xload[i]
        am = s_aa.bump()

        @on(q, "scalar")
        def _(s, sl=sl, xm=xm, i=i, pv=(xn_ready[i - 1] if i >= 1 else 0)):
            if i == 0:
                s.wait_ge(s_c.h, CN)
            s.wait_ge(s_x[sl].h, xm)
            if pv:
                s.wait_ge(s_vv.h, pv)
            s.activation(sq[:], xt[sl][:], AF.Square).then_inc(s_aa.h, 1)

        m_s1 = s_vv.bump(); m_s2 = s_vv.bump(); m_m = s_vv.bump()
        m_msq = s_vv.bump(); m_var = s_vv.bump()

        @on(q, "vector")
        def _(v, sl=sl, xm=xm, am=am, m_s1=m_s1, m_m=m_m, m_msq=m_msq):
            v.wait_ge(s_x[sl].h, xm)
            v.reduce_sum(st1[:], xt[sl][:], axis=AX.X).then_inc(s_vv.h, 1)
            v.wait_ge(s_aa.h, am)
            v.reduce_sum(st2[:], sq[:], axis=AX.X).then_inc(s_vv.h, 1)
            v.wait_ge(s_vv.h, m_s1)
            v.tensor_scalar_mul(stm[:], st1[:], 1.0 / D).then_inc(s_vv.h, 1)
            v.wait_ge(s_vv.h, m_m)
            v.tensor_mul(stv[:], stm[:], stm[:]).then_inc(s_vv.h, 1)
            v.wait_ge(s_vv.h, m_msq)
            v.scalar_tensor_tensor(stv[:], st2[:], 1.0 / D, stv[:],
                                   ALU.mult, ALU.subtract).then_inc(s_vv.h, 1)

        am2 = s_aa.bump()

        @on(q, "scalar")
        def _(s, m_var=m_var):
            s.wait_ge(s_vv.h, m_var)
            s.activation(str_[:], stv[:], AF.Sqrt, bias=cv[:, 0:1]).then_inc(s_aa.h, 1)

        m_rec = s_vv.bump(); m_xn = s_vv.bump()
        xn_ready.append(m_xn)

        @on(q, "vector")
        def _(v, sl=sl, am2=am2, m_rec=m_rec, td=(tile_done[i - 1] if i >= 1 else 0)):
            v.wait_ge(s_aa.h, am2)
            v.reciprocal(str_[:], str_[:]).then_inc(s_vv.h, 1)
            if td:
                v.wait_ge(s_ev.h, td)
            v.wait_ge(s_vv.h, m_rec)
            v.tensor_scalar(xn_tm[:], xt[sl][:], stm[:], str_[:],
                            ALU.subtract, ALU.mult).then_inc(s_vv.h, 1)

        if i + 2 < 8:
            @on(q, "sync")
            def _(s, i=i, sl=sl, m_xn=m_xn):
                s.wait_ge(s_vv.h, m_xn)
                s.dma_start(xt[sl][:], x1_in[128 * (i + 2):128 * (i + 3), :]).then_inc(s_x[sl].h, 16)
            xload.append((sl, s_x[sl].bump(16)))

        ev_base = s_ev.n
        for kt in range(16):
            mmark = s_mm.bump()

            @on(q, "tensor")
            def _(t, kt=kt, i=i, m_xn=m_xn,
                  ev_need=(ev_base + kt if kt >= 1 else (tile_done[i - 1] if i >= 1 else 0))):
                if kt == 0:
                    t.wait_ge(s_vv.h, max(m_xn, CONST_VV))
                if ev_need:
                    t.wait_ge(s_ev.h, ev_need)
                t.transpose(pt[:], xn_tm[:, 128 * kt:128 * (kt + 1)], ident[:]).then_inc(s_mm.h, 1)

            s_ev.bump()

            @on(q, "vector")
            def _(v, kt=kt, i=i, mmark=mmark):
                v.wait_ge(s_mm.h, mmark)
                v.tensor_copy(xn_fm[:, kt * T + 128 * i: kt * T + 128 * (i + 1)],
                              pt[:]).then_inc(s_ev.h, 1)
        tile_done.append(s_ev.n)

    LN_EV = s_ev.n

    @on(q, "sync")
    def _(s, ev=LN_EV):
        s.wait_ge(s_ev.h, ev)
        s.dma_start(xn1_d.rearrange("(n p) t -> p n t", p=128),
                    xn_fm[:].rearrange("p (n t) -> p n t", n=16)).then_inc(s_st.h, 16)
    s_st.bump(16)

    def wstream(w_ap_fn, mtiles, evac_fn, first_extra_ev=0, bank_free=None):
        for blk in range(mtiles):
            for k in range(16):
                gi = gcount["n"]; gcount["n"] += 1
                slot = gi % 2
                need = s_wf.n - 1
                wm = s_ws[slot].bump(16)

                @on(q, "gpsimd")
                def _(g, blk=blk, k=k, slot=slot, need=need, w_ap_fn=w_ap_fn):
                    if need > 0:
                        g.wait_ge(s_wf.h, need)
                    g.dma_start(wbuf[slot][:], w_ap_fn(k, blk)).then_inc(s_ws[slot].h, 16)

                wfm = s_wf.bump()

                @on(q, "tensor")
                def _(t, blk=blk, k=k, slot=slot, wm=wm, first=(k == 0), last=(k == 15),
                      bf=bank_free if k == 0 else None, fev=(first_extra_ev if (blk == 0 and k == 0) else 0)):
                    t.wait_ge(s_ws[slot].h, wm)
                    if fev:
                        t.wait_ge(s_ev.h, fev)
                    if bf:
                        t.wait_ge(bf[0].h, bf[1])
                    for tc in range(2):
                        rhs = xn_fm[:, k * T + 512 * tc: k * T + 512 * (tc + 1)]
                        mm = t.matmul(pb[tc][:], wbuf[slot][:], rhs, start=first, stop=last)
                    mm.then_inc(s_wf.h, 1)
            bank_free = evac_fn(blk, s_wf.n)
        return bank_free

    def qd_evac(blk, blk_wf):
        nb = s_vv.bump(2)

        @on(q, "vector")
        def _(v, blk=blk, blk_wf=blk_wf, base=nb - 2):
            v.wait_ge(s_wf.h, blk_wf)
            for tc in range(2):
                v.tensor_scalar_add(craw[:, blk * T + 512 * tc: blk * T + 512 * tc + 512],
                                    pb[tc][:], bqd[:, blk:blk + 1]).then_inc(s_vv.h, 1)
        return (s_vv, nb)

    bfree = wstream(lambda k, blk: wqd_in[128 * k:128 * (k + 1), 128 * blk:128 * (blk + 1)],
                    8, qd_evac, first_extra_ev=LN_EV)
    QRAW_VV = s_vv.n

    def kd_evac(blk, blk_wf):
        nb = s_vv.bump(2)

        @on(q, "vector")
        def _(v, blk=blk, blk_wf=blk_wf, base=nb - 2):
            v.wait_ge(s_wf.h, blk_wf)
            for tc in range(2):
                v.tensor_scalar_add(c2raw[:, blk * T + 512 * tc: blk * T + 512 * tc + 512],
                                    pb[tc][:], bkd[:, blk:blk + 1]).then_inc(s_vv.h, 1)
        return (s_vv, nb)

    bfree = wstream(lambda k, blk: wkd_in[128 * k:128 * (k + 1), 128 * blk:128 * (blk + 1)],
                    4, kd_evac, bank_free=bfree)
    KRAW_VV = s_vv.n

    def kr_evac(blk, blk_wf):
        nb = s_vv.bump(2)
        sl = blk % 2
        csg = s_cs[sl].n

        @on(q, "vector")
        def _(v, blk=blk, blk_wf=blk_wf, sl=sl, csg=csg):
            v.wait_ge(s_wf.h, blk_wf)
            if csg:
                v.wait_ge(s_cs[sl].h, csg)
            for tc in range(2):
                v.tensor_scalar_add(cstage[sl][:, 512 * tc:512 * tc + 512],
                                    pb[tc][:], bkr[:, blk:blk + 1]).then_inc(s_vv.h, 1)

        csm = s_cs[sl].bump(16)

        @on(q, "sync")
        def _(s, blk=blk, sl=sl, nb=nb):
            s.wait_ge(s_vv.h, nb)
            s.dma_start(lat_kr[128 * blk:128 * (blk + 1), :], cstage[sl][:]).then_inc(s_cs[sl].h, 16)
        return (s_vv, nb)

    bfree = wstream(lambda k, blk: wkr_in[128 * k:128 * (k + 1), 128 * blk:128 * (blk + 1)],
                    8, kr_evac, bank_free=bfree)

    def rms(raw_sb, ntiles, gain_sb, out_ap, raw_vv, drow, nfeat, pr_free=0):
        na = s_aa.bump(2 * ntiles)
        mq = s_mm.bump(2 * ntiles)

        for j in range(ntiles):
            for tc in range(2):
                idx = j * 2 + tc
                aam = na - 2 * ntiles + idx + 1
                mmm = mq - 2 * ntiles + idx + 1

                @on(q, "scalar")
                def _(s, j=j, tc=tc, raw_sb=raw_sb, raw_vv=raw_vv,
                      guard=(mq - 2 * ntiles + idx - 1 if idx >= 2 else 0)):
                    if j == 0 and tc == 0:
                        s.wait_ge(s_vv.h, raw_vv)
                    if guard:
                        s.wait_ge(s_mm.h, guard)
                    s.activation(sqt[tc][:], raw_sb[:, j * T + 512 * tc: j * T + 512 * tc + 512],
                                 AF.Square).then_inc(s_aa.h, 1)

                @on(q, "tensor")
                def _(t, j=j, tc=tc, aam=aam, ntiles=ntiles, prf=(pr_free if idx == 0 else 0)):
                    t.wait_ge(s_aa.h, aam)
                    if prf:
                        t.wait_ge(s_vv.h, prf)
                    t.matmul(pr[0:1, 512 * tc:512 * (tc + 1)], onesf[:, 0:1], sqt[tc][:],
                             start=(j == 0), stop=(j == ntiles - 1)).then_inc(s_mm.h, 1)

        mv = s_vv.bump(2)
        ma = s_aa.bump(1)
        mv2 = s_vv.bump(1)

        pr_read_mark = mv - 1

        @on(q, "vector")
        def _(v, mq=mq, base=mv - 2, nfeat=nfeat):
            v.wait_ge(s_mm.h, mq)
            v.tensor_copy(rrow[:], pr[0:1, :]).then_inc(s_vv.h, 1)
            v.wait_ge(s_vv.h, base + 1)
            v.tensor_scalar_mul(rrow[:], rrow[:], 1.0 / nfeat).then_inc(s_vv.h, 1)

        @on(q, "scalar")
        def _(s, mv=mv):
            s.wait_ge(s_vv.h, mv)
            s.activation(rrow[:], rrow[:], AF.Sqrt, bias=cv[0:1, 1:2]).then_inc(s_aa.h, 1)

        @on(q, "vector")
        def _(v, ma=ma):
            v.wait_ge(s_aa.h, ma)
            v.reciprocal(rrow[:], rrow[:]).then_inc(s_vv.h, 1)

        rdm = s_rd.bump(16)

        @on(q, "sync")
        def _(s, mv2=mv2, drow=drow):
            s.wait_ge(s_vv.h, mv2)
            s.dma_start(rms_d.ap()[drow:drow + 1, :], rrow[:]).then_inc(s_rd.h, 16)

        rbm = s_rb.bump(16)

        @on(q, "gpsimd")
        def _(g, rdm=rdm, drow=drow, guard=0):
            g.wait_ge(s_rd.h, rdm)
            g.dma_start(rbc[:], bass.AP(rms_d, drow * T, [[0, 128], [1, T]])).then_inc(s_rb.h, 16)

        for j in range(ntiles):
            sl = j % 2
            nv = s_vv.bump(2)
            csg = s_cs[sl].n

            @on(q, "vector")
            def _(v, j=j, sl=sl, rbm=rbm, base=nv - 2, raw_sb=raw_sb, gain_sb=gain_sb, csg=csg):
                if j == 0:
                    v.wait_ge(s_rb.h, rbm)
                if csg:
                    v.wait_ge(s_cs[sl].h, csg)
                v.tensor_mul(raw_sb[:, j * T:(j + 1) * T], raw_sb[:, j * T:(j + 1) * T],
                             rbc[:]).then_inc(s_vv.h, 1)
                v.wait_ge(s_vv.h, base + 1)
                v.tensor_scalar_mul(cstage[sl][:], raw_sb[:, j * T:(j + 1) * T],
                                    gain_sb[:, j:j + 1]).then_inc(s_vv.h, 1)

            csm = s_cs[sl].bump(16)

            @on(q, "sync")
            def _(s, j=j, sl=sl, nv=nv, out_ap=out_ap):
                s.wait_ge(s_vv.h, nv)
                s.dma_start(out_ap[128 * j:128 * (j + 1), :], cstage[sl][:]).then_inc(s_cs[sl].h, 16)
        return pr_read_mark

    prm = rms(craw, 8, gq, lat_cq, QRAW_VV, 0, DCQ)
    rms(c2raw, 4, gk, lat_ckv, KRAW_VV, 1, DCKV, pr_free=prm)

    return [(s_cs[0], s_cs[0].n), (s_cs[1], s_cs[1].n), (s_st, s_st.n)]


def emit_attn(C, io):
    nc, q = C.nc, C.q
    ag_lat = io["ag_lat_h"]
    wqu_in = io["wqu"]; bqu_in = io["bqu"]; wkn_in = io["wkn"]; bkn_in = io["bkn"]
    wv_in = io["wv"]; bv_in = io["bv"]; msk_in = io["msk"]
    identf = io["identf"]; identb = io["identb"]; cv = io["cv"]; DYN = io["DYN"]
    attn_d = io["attn_d"]

    sb, ps, sem = C.sb, C.ps, C.sem
    scq = [sb([128, S], BF16) for _ in range(4)]
    cqc = [scq[0], scq[1]]
    ckc = [scq[2][:, 0:4 * T], scq[2][:, 4 * T:8 * T]]
    q_sb = sb([128, 2 * S], BF16)
    k_sb = sb([128, 2 * S], BF16)
    v_sb = sb([128, 64 * 256], BF16)
    wqu = sb([128, 8 * 256], BF16)
    wkn = sb([128, 4 * 128], BF16)
    wv = sb([128, 4 * 256], BF16)
    bqu = sb([128, 2], F32)
    bkn = sb([64, 2], F32)
    bv = sb([128, 256], F32)
    msk = sb([128, 4 * 512], F32)
    bmax = sb([128, 1], F32)
    dden = sb([128, 1], F32)
    ptrb = [sb([128, 512], BF16) for _ in range(2)]
    o_st = [sb([128, 512], BF16) for _ in range(2)]

    psc = [ps([128, 512]) for _ in range(2)]
    ppt = [ps([128, 128], BF16) for _ in range(2)]
    ppo = ps([128, 512])
    pqv = [ps([128, 512]) for _ in range(2)]

    s_c = sem("tc")
    s_cg = sem("tcg")
    s_cq = [sem("tcq0"), sem("tcq1")]
    s_ck = [sem("tck0"), sem("tck1")]
    s_kr = sem("tkr")
    s_mm = sem("tmm")
    s_vv = sem("tvv")
    s_aa = sem("taa")
    s_ev = sem("tev")
    s_os = [sem("tos0"), sem("tos1")]

    C.stage_begin()


    @on(q, "sync")
    def _(s):
        s.dma_start(bqu[:], bqu_in).then_inc(s_c.h, 16)
        s.dma_start(bkn[:], bkn_in).then_inc(s_c.h, 16)
        s.dma_start(bv[:], bv_in).then_inc(s_c.h, 16)
        s.dma_start(msk[:], msk_in).then_inc(s_c.h, 16)
    s_c.bump(64)
    CN = s_c.n

    mident = s_vv.bump()

    @on(q, "vector")
    def _(v, CN=CN):
        v.wait_ge(s_c.h, CN)
        v.tensor_scalar_mul(bmax[:], cv[:, 0:1], 1.0).then_inc(s_vv.h, 1)
    IDENT_VV = mident

    @on(q, "gpsimd")
    def _(g):
        g.dma_start(wqu[:].rearrange("p (n m) -> p n m", n=8),
                    wqu_in.rearrange("(n p) m -> p n m", p=128)).then_inc(s_cg.h, 16)
        g.dma_start(wkn[:].rearrange("p (n m) -> p n m", n=4),
                    wkn_in.rearrange("(n p) m -> p n m", p=128)).then_inc(s_cg.h, 16)
        g.dma_start(wv[:].rearrange("p (n m) -> p n m", n=4),
                    wv_in.rearrange("(n p) m -> p n m", p=128)).then_inc(s_cg.h, 16)
    s_cg.bump(48)
    CG = s_cg.n

    @on(q, "sync")
    def _(s):
        for hh in range(2):
            src_kr = DYN(s, ag_lat, (DCQ + DCKV) * T + hh * (64 * T), 128 * T,
                         [[T, 64], [2560 * T, NCORES], [1, T]])
            s.dma_start(k_sb[64:128, S * hh: S * (hh + 1)].rearrange("p (r t) -> p r t", r=NCORES),
                        src_kr).then_inc(s_kr.h, 16)
    s_kr.bump(32)

    cq_loaded = []
    ck_loaded = []
    for tch in range(2):
        @on(q, "sync")
        def _(s, tch=tch):
            s.dma_start(cqc[tch][:].rearrange("p (n t) -> p n t", n=8),
                        bass.AP(ag_lat, (tch * 2560) * T, [[T, 128], [128 * T, 8], [1, T]])).then_inc(s_cq[tch].h, 16)
            s.dma_start(ckc[tch][:].rearrange("p (n t) -> p n t", n=4),
                        bass.AP(ag_lat, (tch * 2560 + DCQ) * T, [[T, 128], [128 * T, 4], [1, T]])).then_inc(s_ck[tch].h, 16)
        cq_loaded.append(s_cq[tch].bump(16))
        ck_loaded.append(s_ck[tch].bump(16))

    qkv_done = {}
    for tch in range(8):
        sl = tch % 2
        cqm = cq_loaded[tch]
        ckm = ck_loaded[tch]
        for hh in range(2):
            for tc in range(2):
                mmark = s_mm.bump()

                @on(q, "tensor")
                def _(t, tch=tch, sl=sl, hh=hh, tc=tc, cqm=cqm, mmark=mmark,
                      evg=(s_ev.n - 1 if s_ev.n >= 2 else 0), cg=CG):
                    if tch == 0 and hh == 0 and tc == 0:
                        t.wait_ge(s_cg.h, cg)
                    t.wait_ge(s_cq[sl].h, cqm)
                    if evg:
                        t.wait_ge(s_ev.h, evg)
                    for k in range(8):
                        mm = t.matmul(pqv[tc][:], wqu[:, k * 256 + 128 * hh: k * 256 + 128 * (hh + 1)],
                                      cqc[sl][:, k * T + 512 * tc: k * T + 512 * (tc + 1)],
                                      start=(k == 0), stop=(k == 7))
                    mm.then_inc(s_mm.h, 1)

                s_ev.bump()

                @on(q, "vector")
                def _(v, tch=tch, hh=hh, tc=tc, mmark=mmark, cn=CN):
                    if tch == 0 and hh == 0 and tc == 0:
                        v.wait_ge(s_c.h, cn)
                    v.wait_ge(s_mm.h, mmark)
                    v.tensor_copy(q_sb[:, hh * S + tch * T + 512 * tc: hh * S + tch * T + 512 * tc + 512],
                                  pqv[tc][:]).then_inc(s_ev.h, 1)
        for hh in range(2):
            for tc in range(2):
                mmark = s_mm.bump()

                @on(q, "tensor")
                def _(t, tch=tch, sl=sl, hh=hh, tc=tc, ckm=ckm, mmark=mmark,
                      evg=(s_ev.n - 1 if s_ev.n >= 2 else 0)):
                    t.wait_ge(s_ck[sl].h, ckm)
                    if evg:
                        t.wait_ge(s_ev.h, evg)
                    for k in range(4):
                        mm = t.matmul(pqv[tc][0:64, :], wkn[:, k * 128 + 64 * hh: k * 128 + 64 * (hh + 1)],
                                      ckc[sl][:, k * T + 512 * tc: k * T + 512 * (tc + 1)],
                                      start=(k == 0), stop=(k == 3))
                    mm.then_inc(s_mm.h, 1)

                s_ev.bump()

                @on(q, "vector")
                def _(v, tch=tch, hh=hh, tc=tc, mmark=mmark, cn=CN):
                    if tch == 0 and hh == 0 and tc == 0:
                        v.wait_ge(s_c.h, cn)
                    v.wait_ge(s_mm.h, mmark)
                    v.tensor_scalar_add(k_sb[0:64, hh * S + tch * T + 512 * tc: hh * S + tch * T + 512 * tc + 512],
                                        pqv[tc][0:64, :], bkn[:, hh:hh + 1]).then_inc(s_ev.h, 1)
        for tb in range(8):
            gtb = tch * 8 + tb
            mmark = s_mm.bump()

            @on(q, "tensor")
            def _(t, tch=tch, sl=sl, tb=tb, ckm=ckm, mmark=mmark,
                  evg=(s_ev.n - 1 if s_ev.n >= 2 else 0)):
                t.wait_ge(s_ck[sl].h, ckm)
                if evg:
                    t.wait_ge(s_ev.h, evg)
                for k in range(4):
                    mm = t.matmul(pqv[tb % 2][:, 0:256],
                                  ckc[sl][:, k * T + 128 * tb: k * T + 128 * (tb + 1)],
                                  wv[:, k * 256:(k + 1) * 256],
                                  start=(k == 0), stop=(k == 3))
                mm.then_inc(s_mm.h, 1)

            s_ev.bump()

            @on(q, "vector")
            def _(v, gtb=gtb, tb=tb, mmark=mmark):
                v.wait_ge(s_mm.h, mmark)
                v.tensor_add(v_sb[:, gtb * 256:(gtb + 1) * 256], pqv[tb % 2][:, 0:256],
                             bv[:]).then_inc(s_ev.h, 1)

        qkv_done[tch] = s_ev.n
        if tch + 2 < 8:
            @on(q, "sync")
            def _(s, tch=tch, sl=sl, guard=qkv_done[tch]):
                s.wait_ge(s_ev.h, guard)
                s.dma_start(cqc[sl][:].rearrange("p (n t) -> p n t", n=8),
                            bass.AP(ag_lat, ((tch + 2) * 2560) * T, [[T, 128], [128 * T, 8], [1, T]])).then_inc(s_cq[sl].h, 16)
                s.dma_start(ckc[sl][:].rearrange("p (n t) -> p n t", n=4),
                            bass.AP(ag_lat, ((tch + 2) * 2560 + DCQ) * T, [[T, 128], [128 * T, 4], [1, T]])).then_inc(s_ck[sl].h, 16)
            cq_loaded.append(s_cq[sl].bump(16))
            ck_loaded.append(s_ck[sl].bump(16))

    QKV_EV = s_ev.n
    qb = s_vv.bump(2)

    @on(q, "vector")
    def _(v, ev=QKV_EV, base=qb - 2):
        v.wait_ge(s_ev.h, ev)
        v.wait_ge(s_kr.h, 32)
        for hh in range(2):
            v.tensor_scalar_add(q_sb[:, hh * S:(hh + 1) * S], q_sb[:, hh * S:(hh + 1) * S],
                                bqu[:, hh:hh + 1]).then_inc(s_vv.h, 1)
    QB_VV = qb

    oev_track = [0]
    psc_free = {}
    scn = [0]
    tr_free = {}
    ptr_free = {}
    trn = [0]
    kin = [0]
    last_pv = {"mm": 0}
    for hh in range(2):
        for qb in range(NQT // 4):
            nkb = qb + 1
            kend = nkb * 512
            probs_vv = {}
            for j in range(4):
                qt = 4 * qb + j
                prev_tr = last_pv["mm"]
                for kb in range(nkb):
                    si = scn[0]; scn[0] += 1
                    mmark = s_mm.bump()

                    @on(q, "tensor")
                    def _(t, hh=hh, qt=qt, kb=kb, si=si, qbv=QB_VV,
                          first=(hh == 0 and qt == 0 and kb == 0),
                          evg=psc_free.get(si - 2, 0)):
                        if first:
                            t.wait_ge(s_vv.h, qbv)
                        if evg:
                            t.wait_ge(s_ev.h, evg)
                        t.matmul(psc[si % 2][:], q_sb[:, hh * S + 128 * qt: hh * S + 128 * (qt + 1)],
                                 k_sb[:, hh * S + 512 * kb: hh * S + 512 * (kb + 1)],
                                 start=True, stop=True).then_inc(s_mm.h, 1)

                    psc_free[si] = s_ev.bump()

                    @on(q, "vector")
                    def _(v, j=j, kb=kb, si=si, mmark=mmark, mvar=j,
                          diag=(kb == nkb - 1), ptrg=(prev_tr if kb == 0 else 0)):
                        v.wait_ge(s_mm.h, mmark)
                        if ptrg:
                            v.wait_ge(s_mm.h, ptrg)
                        dst = scq[j][:, 512 * kb:512 * (kb + 1)]
                        if diag:
                            v.tensor_add(dst, psc[si % 2][:], msk[:, 512 * mvar:512 * (mvar + 1)]).then_inc(s_ev.h, 1)
                        else:
                            v.tensor_copy(dst, psc[si % 2][:]).then_inc(s_ev.h, 1)
                SC_EV = s_ev.n

                m1 = s_vv.bump(2)

                @on(q, "vector")
                def _(v, j=j, kend=kend, scev=SC_EV, base=m1 - 2):
                    v.wait_ge(s_ev.h, scev)
                    v.reduce_max(bmax[:], scq[j][:, 0:kend], axis=AX.X).then_inc(s_vv.h, 1)
                    v.wait_ge(s_vv.h, base + 1)
                    v.tensor_scalar_mul(bmax[:], bmax[:], -0.0883883476483184).then_inc(s_vv.h, 1)

                ma = s_aa.bump()

                @on(q, "scalar")
                def _(s, j=j, kend=kend, m1=m1):
                    s.wait_ge(s_vv.h, m1)
                    s.activation(scq[j][:, 0:kend], scq[j][:, 0:kend], AF.Exp,
                                 scale=cv[:, 5:6], bias=bmax[:],
                                 accum_out=dden[:]).then_inc(s_aa.h, 1)

                m2r = s_vv.bump(1)
                probs_vv[j] = s_ev.bump()

                @on(q, "vector")
                def _(v, j=j, kend=kend, ma=ma, m2r=m2r):
                    v.wait_ge(s_aa.h, ma)
                    v.reciprocal(dden[:], dden[:]).then_inc(s_vv.h, 1)
                    v.wait_ge(s_vv.h, m2r)
                    v.tensor_scalar_mul(scq[j][:, 0:kend], scq[j][:, 0:kend], dden[:]).then_inc(s_ev.h, 1)

            prev_oev = oev_track[0]
            for ki in range(4 * nkb):
                kig = kin[0]; kin[0] += 1
                ev_marks = []
                for j in range(4):
                    ti = trn[0]; trn[0] += 1
                    tm_ = s_mm.bump()

                    @on(q, "tensor")
                    def _(t, j=j, ki=ki, ti=ti, iv=IDENT_VV,
                          pv=(probs_vv[j] if ki == 0 else 0),
                          first=(hh == 0 and qb == 0 and ki == 0 and j == 0),
                          evg=tr_free.get(ti - 2, 0)):
                        if first:
                            t.wait_ge(s_vv.h, iv)
                        if pv:
                            t.wait_ge(s_ev.h, pv)
                        if evg:
                            t.wait_ge(s_ev.h, evg)
                        t.transpose(ppt[ti % 2][:], scq[j][:, 128 * ki:128 * (ki + 1)],
                                    identb[:]).then_inc(s_mm.h, 1)

                    em = s_ev.bump()
                    tr_free[ti] = em
                    ev_marks.append(em)

                    @on(q, "vector")
                    def _(v, j=j, ti=ti, kig=kig, tm_=tm_,
                          pg=ptr_free.get(kig - 2, 0) if j == 0 else 0):
                        v.wait_ge(s_mm.h, tm_)
                        if pg:
                            v.wait_ge(s_mm.h, pg)
                        v.tensor_copy(ptrb[kig % 2][:, 128 * j:128 * (j + 1)],
                                      ppt[ti % 2][:]).then_inc(s_ev.h, 1)

                pvm = s_mm.bump()
                ptr_free[kig] = pvm

                @on(q, "tensor")
                def _(t, hh=hh, ki=ki, kig=kig, nkb=nkb, evn=ev_marks[-1],
                      og=(prev_oev if ki == 0 else 0)):
                    t.wait_ge(s_ev.h, evn)
                    if og:
                        t.wait_ge(s_ev.h, og)
                    t.matmul(ppo[:], v_sb[:, ki * 256 + 128 * hh: ki * 256 + 128 * (hh + 1)],
                             ptrb[kig % 2][:], start=(ki == 0), stop=(ki == 4 * nkb - 1)).then_inc(s_mm.h, 1)
            PV_MM = s_mm.n
            last_pv["mm"] = PV_MM

            osl = (hh * (NQT // 4) + qb) % 2
            oev = s_ev.bump()
            oev_track[0] = oev
            osg = s_os[osl].n

            @on(q, "vector")
            def _(v, pvm=PV_MM, osl=osl, osg=osg):
                v.wait_ge(s_mm.h, pvm)
                if osg:
                    v.wait_ge(s_os[osl].h, osg)
                v.tensor_copy(o_st[osl][:], ppo[:]).then_inc(s_ev.h, 1)

            om = s_os[osl].bump(16)

            @on(q, "sync")
            def _(s, hh=hh, qb=qb, osl=osl, oev=oev):
                s.wait_ge(s_ev.h, oev)
                s.dma_start(attn_d[128 * hh:128 * (hh + 1), 512 * qb:512 * (qb + 1)],
                            o_st[osl][:]).then_inc(s_os[osl].h, 16)

    return [(s_os[0], s_os[0].n), (s_os[1], s_os[1].n)]


def emit_l1b(C, io):
    nc, q = C.nc, C.q
    ag_attn = io["ag_attn_h"]
    xn1_in = io["xn1_d"]; x1_in = io["x1_d"]
    wow_in = io["wow"]; wW_in = io["wW"]; wA_in = io["wA"]; wB_in = io["wB"]
    bo_in = io["bo"]; bf_in = io["bf"]; e8_in = io["e8"]; rrep_in = io["rrep"]
    r2_in = io["r2"]; s16_in = io["s16"]
    identf = io["identf"]; DYN = io["DYN"]
    out_o = io["out_o"]

    sb, ps, sem = C.sb, C.ps, C.sem
    attn_fm = sb([128, 16 * T], BF16)
    xn_fm = sb([128, 16 * T], BF16)
    arena = sb([128, 16 * T], F32)
    u_sb = sb([128, 8 * T], BF16)
    rbc = sb([128, T], F32)
    vbc = sb([128, T], F32)
    vfm = sb([16, T], F32)
    routes_fm = sb([64, T], F32)
    arw = sb([128, 64], F32)
    kn1 = sb([128, 64], F32)
    mx = sb([128, 1], F32)
    rts = sb([128, 64], F32)
    wbuf = [sb([128, 256], BF16) for _ in range(2)]
    e8 = sb([128, 8], F32)
    rrep = sb([64, 8 * 128], F32)
    r2 = sb([16, 128], F32)
    s16 = sb([128, 16], F32)
    bo = sb([128, 16], F32)
    bf_ = sb([128, 16], F32)
    bsum = sb([128, 16], F32)
    xrow = sb([128, D], F32)
    x1row = sb([128, D], F32)

    pb = [ps([128, 512]) for _ in range(4)]
    pv16 = ps([128, 512])
    pv16b = ps([128, 512])
    psm = ps([128, 128])
    pt = ps([128, 128])

    s_c = sem("bc")
    s_cg = sem("bcg")
    s_ws = [sem("bws0"), sem("bws1")]
    s_wf = sem("bwf")
    s_mm = sem("bmm")
    s_vv = sem("bvv")
    s_aa = sem("baa")
    s_ev = sem("bev")
    s_xl = sem("bxl")
    s_so = sem("bso")

    C.stage_begin()

    gcount = {"n": 0}

    @on(q, "sync")
    def _(s):
        s.dma_start(e8[:], e8_in).then_inc(s_c.h, 16)
        s.dma_start(rrep[:], rrep_in).then_inc(s_c.h, 16)
        s.dma_start(r2[:], r2_in).then_inc(s_c.h, 16)
        s.dma_start(s16[:], s16_in).then_inc(s_c.h, 16)
        s.dma_start(bo[:], bo_in).then_inc(s_c.h, 16)
        s.dma_start(bf_[:], bf_in).then_inc(s_c.h, 16)
    s_c.bump(96)
    CN = s_c.n

    @on(q, "gpsimd")
    def _(g):
        src_at = DYN(g, ag_attn, 0, T,
                     [[S, 128], [256 * S, 8], [128 * S, 2], [1, T]])
        g.dma_start(attn_fm[:].rearrange("p (r k2 t) -> p r k2 t", r=8, k2=2),
                    src_at).then_inc(s_cg.h, 16)
        g.dma_start(xn_fm[:].rearrange("p (n t) -> p n t", n=16),
                    xn1_in.rearrange("(n p) t -> p n t", p=128)).then_inc(s_cg.h, 16)
    s_cg.bump(32)
    CG = s_cg.n

    mb = s_vv.bump()

    @on(q, "vector")
    def _(v, cn=CN):
        v.wait_ge(s_c.h, cn)
        v.tensor_add(bsum[:], bo[:], bf_[:]).then_inc(s_vv.h, 1)
    BSUM_VV = mb

    bank_free = None
    for blk in range(8):
        for k in range(16):
            gi = gcount["n"]; gcount["n"] += 1
            slot = gi % 2
            need = s_wf.n - 1
            wm = s_ws[slot].bump(16)

            @on(q, "gpsimd")
            def _(g, blk=blk, k=k, slot=slot, need=need):
                if need > 0:
                    g.wait_ge(s_wf.h, need)
                g.dma_start(wbuf[slot][:, 0:128],
                            wA_in[128 * k:128 * (k + 1), 128 * blk:128 * (blk + 1)]).then_inc(s_ws[slot].h, 16)

            wfm = s_wf.bump()

            @on(q, "tensor")
            def _(t, blk=blk, k=k, slot=slot, wm=wm, first=(k == 0), last=(k == 15),
                  cg=CG, bfr=bank_free if k == 0 else None):
                t.wait_ge(s_ws[slot].h, wm)
                if blk == 0 and k == 0:
                    t.wait_ge(s_cg.h, cg)
                if bfr:
                    t.wait_ge(bfr[0].h, bfr[1])
                for tc in range(2):
                    rhs = xn_fm[:, k * T + 512 * tc: k * T + 512 * (tc + 1)]
                    mm = t.matmul(pb[tc][:], wbuf[slot][:, 0:128], rhs, start=first, stop=last)
                mm.then_inc(s_wf.h, 1)

        blk_wf = s_wf.n
        nv = s_vv.bump(2)
        na = s_aa.bump(2)

        @on(q, "vector")
        def _(v, blk=blk, blk_wf=blk_wf):
            v.wait_ge(s_wf.h, blk_wf)
            for tc in range(2):
                v.tensor_copy(arena[:, blk * T + 512 * tc: blk * T + 512 * tc + 512],
                              pb[tc][:]).then_inc(s_vv.h, 1)

        @on(q, "scalar")
        def _(s, blk=blk, nv=nv):
            s.wait_ge(s_vv.h, nv)
            for tc in range(2):
                s.activation(arena[:, 8 * T + blk * T + 512 * tc: 8 * T + blk * T + 512 * tc + 512],
                             arena[:, blk * T + 512 * tc: blk * T + 512 * tc + 512],
                             AF.Square).then_inc(s_aa.h, 1)
        bank_free = (s_aa, na)
    VEC_AA = s_aa.n

    ROUTES_EV = 0
    prev_arw = 0
    for i in range(8):
        mmark = s_mm.bump()

        @on(q, "tensor")
        def _(t, i=i, mmark=mmark, va=VEC_AA, cn=CN, prf=prev_arw):
            if i == 0:
                t.wait_ge(s_aa.h, va)
                t.wait_ge(s_c.h, cn)
            if prf:
                t.wait_ge(s_vv.h, prf)
            for at in range(8):
                mm = t.matmul(psm[0:128, 8 * at:8 * at + 8],
                              arena[:, 8 * T + at * T + 128 * i: 8 * T + at * T + 128 * (i + 1)],
                              e8[:], start=True, stop=True)
            mm.then_inc(s_mm.h, 1)

        nv = s_vv.bump(12)

        @on(q, "vector")
        def _(v, i=i, mmark=mmark, base=nv - 12):
            k = [base]
            def stp(ins):
                ins.then_inc(s_vv.h, 1)
                k[0] += 1
                v.wait_ge(s_vv.h, k[0])
            v.wait_ge(s_mm.h, mmark)
            stp(v.tensor_copy(arw[:], psm[0:128, 0:64]))
            stp(v.tensor_copy(kn1[:], arw[:]))
            for it in range(3):
                stp(v.reduce_max(mx[:], kn1[:], axis=AX.X))
                stp(v.tensor_scalar(rts[:], kn1[:], mx[:], -1e30, ALU.is_ge, ALU.mult))
                stp(v.tensor_add(kn1[:], kn1[:], rts[:]))
            stp(v.reduce_max(mx[:], kn1[:], axis=AX.X))
        prev_arw = nv - 11

        nr = s_vv.bump(1)

        @on(q, "vector")
        def _(v, i=i, nv=nv):
            v.wait_ge(s_vv.h, nv)
            v.tensor_scalar(rts[:], arw[:], mx[:], 0.25, ALU.is_ge, ALU.mult).then_inc(s_vv.h, 1)

        tm_ = s_mm.bump()
        tev = s_ev.bump()

        @on(q, "tensor")
        def _(t, i=i, nr=nr, pfree=(tev - 1 if i >= 1 else 0)):
            t.wait_ge(s_vv.h, nr)
            if pfree:
                t.wait_ge(s_ev.h, pfree)
            t.transpose(pt[0:64, 0:128], rts[:, 0:64], identf[:]).then_inc(s_mm.h, 1)

        @on(q, "vector")
        def _(v, i=i, tm_=tm_):
            v.wait_ge(s_mm.h, tm_)
            v.tensor_copy(routes_fm[:, 128 * i:128 * (i + 1)], pt[0:64, 0:128]).then_inc(s_ev.h, 1)
    ROUTES_EV = s_ev.n

    WV_MARKS = []
    vacc_last = None
    for ft in range(8):
        rm = s_mm.bump()

        @on(q, "tensor")
        def _(t, ft=ft, rm=rm, rev=ROUTES_EV, pf=(WV_MARKS[ft - 1][0] if ft >= 1 else 0)):
            if ft == 0:
                t.wait_ge(s_ev.h, rev)
            if pf:
                t.wait_ge(s_vv.h, pf)
            for tc in range(2):
                mm = t.matmul([pv16, pv16b][tc][:], rrep[:, 128 * ft:128 * (ft + 1)],
                              routes_fm[:, 512 * tc:512 * (tc + 1)], start=True, stop=True)
            mm.then_inc(s_mm.h, 1)

        nv = s_vv.bump(4)

        @on(q, "vector")
        def _(v, ft=ft, rm=rm, base=nv - 4):
            k = [base]
            def stp(ins):
                ins.then_inc(s_vv.h, 1)
                k[0] += 1
                v.wait_ge(s_vv.h, k[0])
            v.wait_ge(s_mm.h, rm)
            stp(v.tensor_copy(rbc[:, 0:512], pv16[:]))
            stp(v.tensor_copy(rbc[:, 512:1024], pv16b[:]))
            stp(v.tensor_mul(arena[:, 8 * T + ft * T: 8 * T + ft * T + 512],
                             arena[:, ft * T: ft * T + 512], rbc[:, 0:512]))
            stp(v.tensor_mul(arena[:, 8 * T + ft * T + 512: 8 * T + (ft + 1) * T],
                             arena[:, ft * T + 512: (ft + 1) * T], rbc[:, 512:1024]))
        WV_MARKS.append((nv, rm))

        vm = s_mm.bump()

        @on(q, "tensor")
        def _(t, ft=ft, nv=nv):
            t.wait_ge(s_vv.h, nv)
            for tc in range(2):
                mm = t.matmul(pb[2 + tc][0:16, :], s16[:],
                              arena[:, 8 * T + ft * T + 512 * tc: 8 * T + ft * T + 512 * (tc + 1)],
                              start=(ft == 0), stop=(ft == 7))
            mm.then_inc(s_mm.h, 1)
        vacc_last = s_mm.n

    nvf = s_vv.bump(2)

    @on(q, "vector")
    def _(v, vm=vacc_last, base=nvf - 2):
        v.wait_ge(s_mm.h, vm)
        v.tensor_copy(vfm[:, 0:512], pb[2][0:16, :]).then_inc(s_vv.h, 1)
        v.tensor_copy(vfm[:, 512:1024], pb[3][0:16, :]).then_inc(s_vv.h, 1)
    VFM_VV = nvf

    vbm = s_mm.bump()

    @on(q, "tensor")
    def _(t, vv=VFM_VV, vbm=vbm):
        t.wait_ge(s_vv.h, vv)
        for tc in range(2):
            mm = t.matmul([pv16, pv16b][tc][:], r2[:], vfm[:, 512 * tc:512 * (tc + 1)],
                          start=True, stop=True)
        mm.then_inc(s_mm.h, 1)

    nvb = s_vv.bump(2)

    @on(q, "vector")
    def _(v, vbm=vbm, base=nvb - 2):
        v.wait_ge(s_mm.h, vbm)
        v.tensor_copy(vbc[:, 0:512], pv16[:]).then_inc(s_vv.h, 1)
        v.tensor_copy(vbc[:, 512:1024], pv16b[:]).then_inc(s_vv.h, 1)
    VBC_VV = nvb

    u_last = 0
    for ft in range(8):
        rm = s_mm.bump()

        @on(q, "tensor")
        def _(t, ft=ft, rm=rm, vb=VBC_VV, pf=u_last):
            if ft == 0:
                t.wait_ge(s_vv.h, vb)
            if pf:
                t.wait_ge(s_vv.h, pf)
            for tc in range(2):
                mm = t.matmul([pv16, pv16b][tc][:], rrep[:, 128 * ft:128 * (ft + 1)],
                              routes_fm[:, 512 * tc:512 * (tc + 1)], start=True, stop=True)
            mm.then_inc(s_mm.h, 1)

        nv = s_vv.bump(4)

        @on(q, "vector")
        def _(v, ft=ft, rm=rm, base=nv - 4):
            k = [base]
            def stp(ins):
                ins.then_inc(s_vv.h, 1)
                k[0] += 1
                v.wait_ge(s_vv.h, k[0])
            v.wait_ge(s_mm.h, rm)
            stp(v.tensor_copy(rbc[:, 0:512], pv16[:]))
            stp(v.tensor_copy(rbc[:, 512:1024], pv16b[:]))
            stp(v.tensor_mul(u_sb[:, ft * T: ft * T + 512], rbc[:, 0:512], vbc[:, 0:512]))
            stp(v.tensor_mul(u_sb[:, ft * T + 512:(ft + 1) * T], rbc[:, 512:1024], vbc[:, 512:1024]))
        u_last = nv
    U_VV = u_last

    bank_free = None
    ev_marks = {}
    for blk in range(8):
        for ks in range(40):
            gi = gcount["n"]; gcount["n"] += 1
            slot = gi % 2
            need = s_wf.n - 1
            wm = s_ws[slot].bump(16)

            @on(q, "gpsimd")
            def _(g, blk=blk, ks=ks, slot=slot, need=need):
                if need > 0:
                    g.wait_ge(s_wf.h, need)
                if ks < 16:
                    src_ap = wow_in[128 * ks:128 * (ks + 1), 256 * blk:256 * (blk + 1)]
                elif ks < 32:
                    src_ap = wW_in[128 * (ks - 16):128 * (ks - 15), 256 * blk:256 * (blk + 1)]
                else:
                    src_ap = wB_in[128 * (ks - 32):128 * (ks - 31), 256 * blk:256 * (blk + 1)]
                g.dma_start(wbuf[slot][:], src_ap).then_inc(s_ws[slot].h, 16)

            wfm = s_wf.bump()

            @on(q, "tensor")
            def _(t, blk=blk, ks=ks, slot=slot, wm=wm, first=(ks == 0), last=(ks == 39),
                  uv=U_VV, bfr=bank_free if ks == 0 else None,
                  evg=(ev_marks.get(blk - 1, 0) if ks == 0 else 0)):
                t.wait_ge(s_ws[slot].h, wm)
                if blk == 0 and ks == 0:
                    t.wait_ge(s_vv.h, uv)
                if evg:
                    t.wait_ge(s_ev.h, evg)
                if ks < 16:
                    rhs_base = attn_fm
                    kk = ks
                elif ks < 32:
                    rhs_base = xn_fm
                    kk = ks - 16
                else:
                    rhs_base = u_sb
                    kk = ks - 32
                for j in range(2):
                    for tc in range(2):
                        rhs = rhs_base[:, kk * T + 512 * tc: kk * T + 512 * (tc + 1)]
                        mm = t.matmul(pb[2 * j + tc][:], wbuf[slot][:, 128 * j:128 * (j + 1)],
                                      rhs, start=first, stop=last)
                mm.then_inc(s_wf.h, 1)

        blk_wf = s_wf.n
        em = s_ev.bump(4)
        ev_marks[blk] = em

        @on(q, "vector")
        def _(v, blk=blk, blk_wf=blk_wf, bs=BSUM_VV):
            v.wait_ge(s_wf.h, blk_wf)
            if blk == 0:
                v.wait_ge(s_vv.h, bs)
            for j in range(2):
                dtl = 2 * blk + j
                for tc in range(2):
                    v.tensor_scalar_add(arena[:, dtl * T + 512 * tc: dtl * T + 512 * tc + 512],
                                        pb[2 * j + tc][:], bsum[:, dtl:dtl + 1]).then_inc(s_ev.h, 1)
    MAIN_EV = s_ev.n

    prev_row_ev = 0
    for i in range(8):
        xm = s_xl.bump(16)

        @on(q, "sync")
        def _(s, i=i, xm=xm, sog=s_so.n, pre=prev_row_ev):
            if pre:
                s.wait_ge(s_ev.h, pre)
            if sog:
                s.wait_ge(s_so.h, sog)
            s.dma_start(xrow[:], x1_in[128 * i:128 * (i + 1), :]).then_inc(s_xl.h, 16)

        for kt in range(16):
            mmark = s_mm.bump()

            @on(q, "tensor")
            def _(t, i=i, kt=kt, fev=MAIN_EV, ev=(s_ev.n if (i or kt) else 0)):
                if i == 0 and kt == 0:
                    t.wait_ge(s_ev.h, fev)
                if ev:
                    t.wait_ge(s_ev.h, ev)
                t.transpose(pt[:], arena[:, kt * T + 128 * i: kt * T + 128 * (i + 1)],
                            identf[:]).then_inc(s_mm.h, 1)

            s_ev.bump()

            @on(q, "vector")
            def _(v, i=i, kt=kt, xm=xm, mmark=mmark, sog=(s_so.n if kt == 0 else 0)):
                v.wait_ge(s_mm.h, mmark)
                if kt == 0:
                    v.wait_ge(s_xl.h, xm)
                    if sog:
                        v.wait_ge(s_so.h, sog)
                v.tensor_add(x1row[:, 128 * kt:128 * (kt + 1)], pt[:],
                             xrow[:, 128 * kt:128 * (kt + 1)]).then_inc(s_ev.h, 1)
        prev_row_ev = s_ev.n

        @on(q, "sync")
        def _(s, i=i, row_ev=prev_row_ev):
            s.wait_ge(s_ev.h, row_ev)
            s.dma_start(out_o[128 * i:128 * (i + 1), :], x1row[:]).then_inc(s_so.h, 16)
        s_so.bump(16)

    return [(s_so, s_so.n)]


def build_fused(stages=(1,2,3,4,5,6)):
    nc = bass.Bass("TRN2", target_bir_lowering=False)
    E = lambda n, sh, dt: nc.dram_tensor(n, sh, dt, kind="ExternalInput").ap()
    io = {}
    io["x_ext"] = E("x_ext", [TH, D], F32)
    io["w_in"] = E("w_in", [D, 2 * DI], F32)
    io["cw_in"] = E("cw_in", [128, 128], F32)
    io["cb_in"] = E("cb_in", [128, 32], F32)
    io["w_xp"] = E("w_xp", [DI, 160], F32)
    io["w_dtp"] = E("w_dtp", [DTR, DI], F32)
    io["dtb_in"] = E("dtb_in", [128, 32], F32)
    io["alog_in"] = E("alog_in", [128, NDT * DS], F32)
    io["d_in"] = E("d_in", [128, NDT], F32)
    io["w_op"] = E("w_op", [DI, D], F32)
    io["wg"] = E("wg", [E0, D, F0], F32)
    io["wu"] = E("wu", [E0, D, F0], F32)
    io["wd"] = E("wd", [E0, F0, D], F32)
    io["ga"] = E("ga", [NL, D, R0], F32)
    io["gb"] = E("gb", [NL, R0, F0], F32)
    io["ua"] = E("ua", [NL, D, R0], F32)
    io["ub"] = E("ub", [NL, R0, F0], F32)
    io["da"] = E("da", [NL, F0, R0], F32)
    io["db"] = E("db", [NL, R0, D], F32)
    io["wr"] = E("wr", [D, E0], F32)
    io["tr_in"] = E("tr_in", [128, 1], F32)
    io["wqd"] = E("wqd", [D, DCQ], F32)
    io["wkd"] = E("wkd", [D, DCKV], F32)
    io["wkr"] = E("wkr", [D, H * 64], F32)
    io["bqd"] = E("bqd", [128, 8], F32)
    io["bkd"] = E("bkd", [128, 4], F32)
    io["bkr"] = E("bkr", [128, 8], F32)
    io["gq"] = E("gq", [128, 8], F32)
    io["gk"] = E("gk", [128, 4], F32)
    io["wqu"] = E("wqu", [DCQ, 256], F32)
    io["bqu"] = E("bqu", [128, 2], F32)
    io["wkn"] = E("wkn", [DCKV, 128], F32)
    io["bkn"] = E("bkn", [64, 2], F32)
    io["wv"] = E("wv", [DCKV, 256], F32)
    io["bv"] = E("bv", [128, 256], F32)
    io["msk"] = E("msk", [128, 4 * 512], F32)
    io["wow"] = E("wow", [D, D], F32)
    io["wW"] = E("wW", [D, D], F32)
    io["wA"] = E("wA", [D, A1 * R1], F32)
    io["wB"] = E("wB", [A1 * R1, D], F32)
    io["bo"] = E("bo", [128, 16], F32)
    io["bf"] = E("bf", [128, 16], F32)
    io["e8"] = E("e8", [128, 8], F32)
    io["rrep"] = E("rrep", [64, 8 * 128], F32)
    io["r2"] = E("r2", [16, 128], F32)
    io["s16"] = E("s16", [128, 16], F32)
    io["ident_in"] = E("ident_in", [128, 128], F32)
    io["cv_in"] = E("cv_in", [128, 8], F32)
    io["onesf_in"] = E("onesf_in", [128, 1], F32)
    io["rank_in"] = E("rank_in", [1, 1], U32)
    io["out_o"] = nc.dram_tensor("out_o", [T, D], F32, kind="ExternalOutput").ap()

    pre_f32_h = nc.dram_tensor("pre_f32_d", [PRE_ROWS, T], F32)
    pre_bf_h = nc.dram_tensor("pre_bf_d", [DI, T], BF16)
    ag_pre_f32_h = nc.dram_tensor("ag_pre_f32_d", [NCORES * PRE_ROWS, T], F32)
    ag_pre_bf_h = nc.dram_tensor("ag_pre_bf_d", [NCORES * DI, T], BF16)
    xn_h = nc.dram_tensor("xn_d", [D, T], BF16)
    z_h = nc.dram_tensor("z_d", [DI, T], BF16)
    y_h = nc.dram_tensor("y_d", [CPC, S], BF16)
    ag_y_h = nc.dram_tensor("ag_y_d", [DI, S], BF16)
    x1_h = nc.dram_tensor("x1_d", [T, D], F32)
    xn1_h = nc.dram_tensor("xn1_d", [D, T], BF16)
    lat_h = nc.dram_tensor("lat_d", [2560, T], BF16)
    ag_lat_h = nc.dram_tensor("ag_lat_d", [NCORES * 2560, T], BF16)
    attn_h = nc.dram_tensor("attn_d", [256, S], BF16)
    ag_attn_h = nc.dram_tensor("ag_attn_d", [D, S], BF16)
    wt_h = nc.dram_tensor("wt_d", [E0, T], F32)
    y_loc_h = nc.dram_tensor("y_loc_d", [DI, T], BF16)
    rms_h = nc.dram_tensor("rms_d", [2, T], F32)

    io["pre_f32_d"] = pre_f32_h.ap()
    io["pre_bf_d"] = pre_bf_h.ap()
    io["ag_pre_f32_d"] = ag_pre_f32_h
    io["ag_pre_bf_d"] = ag_pre_bf_h
    io["xn_d"] = xn_h.ap()
    io["z_d"] = z_h.ap()
    io["y_d"] = y_h.ap()
    io["ag_y_d"] = ag_y_h
    io["x1_d"] = x1_h.ap()
    io["xn1_d"] = xn1_h.ap()
    io["lat_d"] = lat_h.ap()
    io["ag_lat_h"] = ag_lat_h
    io["attn_d"] = attn_h.ap()
    io["ag_attn_h"] = ag_attn_h
    io["wt_d"] = wt_h
    io["y_loc_h"] = y_loc_h
    io["rms_d"] = rms_h

    with ExitStack() as es:
        C = Ctx(nc, es)
        identf = C.sb([128, 128], F32)
        identb = C.sb([128, 128], BF16)
        cv = C.sb([128, 8], F32)
        onesf = C.sb([128, 1], F32)
        rk_sb_t = es.enter_context(nc.sbuf_tensor("rk_sb", [1, 1], U32))
        rk_sb = rk_sb_t[0:1, 0:1]
        C.cur = 204
        C.base = C.cur
        io["identf"] = identf
        io["identb"] = identb
        io["cv"] = cv
        io["onesf"] = onesf

        s_gc = C.sem("gc_dma")
        s_gv = C.sem("gc_vec")
        s_gr = C.sem("gc_rank")
        io["s_gc"] = s_gv
        io["gconst_vv"] = 1
        io["gconst_dma"] = 1

        dynregs = {}

        def DYN(eng, tensor, static, scale, dims):
            key = id(eng)
            if key not in dynregs:
                rr = eng.alloc_register("rkr")
                ro = eng.alloc_register("rko")
                eng.reg_load(rr, rk_sb)
                dynregs[key] = (rr, ro)
            rr, ro = dynregs[key]
            eng.reg_mul(ro, rr, scale)
            if static:
                eng.reg_add(ro, ro, static)
            return bass.AP(tensor, ro, dims)
        io["DYN"] = DYN
        io["RK"] = None
        io["RKS"] = None

        q = C.q

        @on(q, "sync")
        def _(s):
            s.dma_start(identf[:], io["ident_in"]).then_inc(s_gc.h, 16)
            s.dma_start(cv[:], io["cv_in"]).then_inc(s_gc.h, 16)
            s.dma_start(onesf[:], io["onesf_in"]).then_inc(s_gc.h, 16)
        s_gc.bump(48)

        @on(q, "gpsimd")
        def _(g):
            g.dma_start(rk_sb, io["rank_in"]).then_inc(s_gr.h, 16)
        s_gr.bump(16)

        @on(q, "vector")
        def _(v):
            v.wait_ge(s_gc.h, 48)
            v.tensor_copy(identb[:], identf[:]).then_inc(s_gv.h, 1)
        s_gv.bump(1)

        for e in ENGINES:
            @on(q, e)
            def _(eng):
                eng.wait_ge(s_gc.h, 48)
                eng.wait_ge(s_gr.h, 16)
                eng.wait_ge(s_gv.h, 1)

        block = es.enter_context(nc.Block())
        rg = [list(range(NCORES))]

        def emit_ag(name, pairs, waits):
            out = []
            sems = [C.sem("ag_" + name + str(i)) for i in range(len(pairs))]

            @on(q, "gpsimd")
            def _(g, pairs=pairs, waits=list(waits), sems=sems):
                for sem, n in waits:
                    if n:
                        g.wait_ge(sem.h, n)
                for (src, dst), ss in zip(pairs, sems):
                    g.collective_compute("AllGather", ALU.bypass, replica_groups=rg,
                                         ins=[src], outs=[dst]).then_inc(ss.h)
            for ss in sems:
                ss.bump(1)
                out.append((ss, 1))
            return out

        if 1 in stages:
            w1 = emit_pre(C, io)
            ag1 = emit_ag("pre", [(pre_f32_h.ap(), ag_pre_f32_h.ap()),
                                  (pre_bf_h.ap(), ag_pre_bf_h.ap())], w1)
            C.stage_end(w1 + ag1)
            C.cur = C.base; C.pcur = 0

        if 2 in stages:
            w2 = emit_scan(C, io)
            ag2 = emit_ag("y", [(y_h.ap(), ag_y_h.ap())], w2)
            C.stage_end(w2 + ag2)
            C.cur = C.base; C.pcur = 0

        if 3 in stages:
            w3 = emit_post0(C, io)
            C.stage_end(w3)
            C.cur = C.base; C.pcur = 0

        if 4 in stages:
            w4 = emit_l1a(C, io)
            ag4 = emit_ag("lat", [(lat_h.ap(), ag_lat_h.ap())], w4)
            C.stage_end(w4 + ag4)
            C.cur = C.base; C.pcur = 0

        if 5 in stages:
            w5 = emit_attn(C, io)
            ag5 = emit_ag("attn", [(attn_h.ap(), ag_attn_h.ap())], w5)
            C.stage_end(w5 + ag5)
            C.cur = C.base; C.pcur = 0

        if 6 in stages:
            w6 = emit_l1b(C, io)
            C.stage_end(w6)

        q.emit(block)
    return nc



def _t(a):
    return np.ascontiguousarray(np.asarray(a).T)


def _f32(a):
    return np.ascontiguousarray(np.asarray(a), dtype=np.float32)


def fused_inputs(inputs):
    x = _f32(inputs["L0_attn_hidden_states"]).reshape(S, D)
    xpad = np.concatenate([np.zeros((3, D), np.float32), x], axis=0)
    cw = _f32(inputs["L0_attn_conv_w"])
    cb = _f32(inputs["L0_attn_conv_b"])
    dtbv = _f32(inputs["L0_attn_dt_proj_b"])
    alog = _f32(inputs["L0_attn_A_log"])
    dd = _f32(inputs["L0_attn_D"])
    temp = float(np.asarray(inputs["L0_ffn_temperature"]).ravel()[0])
    cvals = np.zeros((128, 8), np.float32)
    cvals[:, 0] = LN_EPS
    cvals[:, 1] = RMS_EPS
    cvals[:, 2] = -GOLDEN_CENTER
    cvals[:, 3] = -10.0
    cvals[:, 4] = -1.0 / 0.3
    cvals[:, 5] = 1.0 / np.sqrt(128.0)
    ident = np.eye(128, dtype=np.float32)
    onesf = np.ones((128, 1), np.float32)
    msk = np.zeros((128, 4 * 512), np.float32)
    ii = np.arange(128)[:, None]
    jj = np.arange(512)[None, :]
    for mvar in range(4):
        msk[:, 512 * mvar:512 * (mvar + 1)] = np.where(jj <= ii + 128 * mvar, 0.0, -30000.0)
    A_ = _f32(inputs["L1_ffn_A"])
    B_ = _f32(inputs["L1_ffn_B"])
    pp = np.arange(128)
    e8 = (pp[:, None] // 16 == np.arange(8)[None, :]).astype(np.float32)
    rrep = np.zeros((64, 8 * 128), np.float32)
    for ft in range(8):
        for p in range(128):
            rrep[ft * 8 + p // 16, ft * 128 + p] = 1.0
    r2 = (np.arange(16)[:, None] == pp[None, :] % 16).astype(np.float32)
    s16 = (pp[:, None] % 16 == np.arange(16)[None, :]).astype(np.float32)
    wqu_f = _t(inputs["L1_attn_q_up_w"])
    bqu_f = _f32(inputs["L1_attn_q_up_b"])
    wkv_f = _t(inputs["L1_attn_kv_up_w"])
    bkv_f = _f32(inputs["L1_attn_kv_up_b"])

    shared = {
        "w_in": _t(inputs["L0_attn_in_proj_w"]),
        "cw_in": np.ascontiguousarray(cw.reshape(32, 128, 4).transpose(1, 0, 2).reshape(128, 128)),
        "cb_in": np.ascontiguousarray(cb.reshape(32, 128).T),
        "w_xp": _t(inputs["L0_attn_x_proj_w"]),
        "w_dtp": _t(inputs["L0_attn_dt_proj_w"]),
        "dtb_in": np.ascontiguousarray(dtbv.reshape(32, 128).T),
        "w_op": _t(inputs["L0_attn_out_proj_w"]),
        "wg": _f32(inputs["L0_ffn_Wg"]), "wu": _f32(inputs["L0_ffn_Wu"]),
        "wd": _f32(inputs["L0_ffn_Wd"]),
        "ga": _f32(inputs["L0_ffn_Ga"]), "gb": _f32(inputs["L0_ffn_Gb"]),
        "ua": _f32(inputs["L0_ffn_Ua"]), "ub": _f32(inputs["L0_ffn_Ub"]),
        "da": _f32(inputs["L0_ffn_Da"]), "db": _f32(inputs["L0_ffn_Db"]),
        "wr": _f32(inputs["L0_ffn_Wr"]),
        "tr_in": np.full((128, 1), 1.0 / temp, np.float32),
        "wqd": _t(inputs["L1_attn_q_down_w"]),
        "wkd": _t(inputs["L1_attn_kv_down_w"]),
        "wkr": _t(inputs["L1_attn_k_rope_w"]),
        "bqd": np.ascontiguousarray(_f32(inputs["L1_attn_q_down_b"]).reshape(8, 128).T),
        "bkd": np.ascontiguousarray(_f32(inputs["L1_attn_kv_down_b"]).reshape(4, 128).T),
        "bkr": np.ascontiguousarray(_f32(inputs["L1_attn_k_rope_b"]).reshape(8, 128).T),
        "gq": np.ascontiguousarray(_f32(inputs["L1_attn_q_norm_w"]).reshape(8, 128).T),
        "gk": np.ascontiguousarray(_f32(inputs["L1_attn_kv_norm_w"]).reshape(4, 128).T),
        "msk": msk,
        "wow": _t(inputs["L1_attn_out_w"]),
        "wW": _t(inputs["L1_ffn_W"]),
        "wA": np.ascontiguousarray(A_.transpose(2, 0, 1).reshape(D, A1 * R1)),
        "wB": np.ascontiguousarray(B_.transpose(0, 2, 1).reshape(A1 * R1, D)),
        "bo": np.ascontiguousarray(_f32(inputs["L1_attn_out_b"]).reshape(16, 128).T),
        "bf": np.ascontiguousarray(_f32(inputs["L1_ffn_bias"]).reshape(16, 128).T),
        "e8": e8, "rrep": rrep, "r2": r2, "s16": s16,
        "ident_in": ident, "cv_in": cvals, "onesf_in": onesf,
    }
    maps = []
    for cidx in range(NCORES):
        h0 = 2 * cidx
        wqu = np.ascontiguousarray(wqu_f[:, 128 * h0:128 * (h0 + 2)])
        bqu = np.ascontiguousarray(bqu_f[128 * h0:128 * (h0 + 2)].reshape(2, 128).T)
        kn_cols, v_cols, bkn_cols, bv_cols = [], [], [], []
        for h in (h0, h0 + 1):
            base = 192 * h
            kn_cols.append(wkv_f[:, base:base + 64])
            v_cols.append(wkv_f[:, base + 64:base + 192])
            bkn_cols.append(bkv_f[base:base + 64])
            bv_cols.append(bkv_f[base + 64:base + 192])
        rows = slice(cidx * CPC, (cidx + 1) * CPC)
        al = alog[rows].reshape(NDT, 128, DS).transpose(1, 0, 2).reshape(128, NDT * DS)
        dl = dd[rows].reshape(NDT, 128).T
        m = dict(shared)
        m.update({
            "x_ext": np.ascontiguousarray(xpad[cidx * T: cidx * T + TH]),
            "alog_in": np.ascontiguousarray(al),
            "d_in": np.ascontiguousarray(dl),
            "wqu": wqu, "bqu": bqu,
            "wkn": np.ascontiguousarray(np.concatenate(kn_cols, axis=1)),
            "bkn": np.ascontiguousarray(np.stack(bkn_cols, axis=1)),
            "wv": np.ascontiguousarray(np.concatenate(v_cols, axis=1)),
            "bv": np.ascontiguousarray(np.broadcast_to(np.concatenate(bv_cols)[None, :], (128, 256))),
            "rank_in": np.array([[cidx]], np.uint32),
        })
        maps.append(m)
    return maps


def kernel(**inputs):
    nc = build_fused()
    maps = fused_inputs(inputs)
    res = run_bass_kernel_spmd(nc, maps, core_ids=list(range(NCORES)))
    out = np.concatenate([np.asarray(r["out_o"], dtype=np.float32) for r in res.results], axis=0)
    return out.reshape(1, S, D)
